# version 21
# baseline (speedup 1.0000x reference)
import contextlib
import numpy as np
import ml_dtypes
import concourse.bass as bass
import concourse.mybir as mybir
from concourse.bass_utils import run_bass_kernel_spmd

F32 = mybir.dt.float32
BF16 = mybir.dt.bfloat16
AF = mybir.ActivationFunctionType
ALU = mybir.AluOpType
AX = mybir.AxisListType

S, D, H, DH, DV, DHID = 4096, 1024, 16, 64, 256, 4096
NT, NQB, NFQ, NCD = 32, 8, 16, 8
VST = 260

ENGS = ("sync", "gpsimd", "vector", "scalar", "tensor")
SEMS = ("dc", "dx0", "dx1", "dt0", "dt1", "mks",
        "gwk", "gwf", "gcn", "gv0", "gv1", "gb0", "gb1", "mkg",
        "so0", "so1", "sg", "sgo0", "sgo1", "sh0", "sh1", "sout0", "sout1",
        "t", "v", "s", "g", "mkt")
QSEMS = {"sync": ("dc", "dx0", "dx1", "dt0", "dt1", "sh0", "sh1"),
         "gpsimd": ("gwk", "gwf", "gcn", "gv0", "gv1", "gb0", "gb1"),
         "scalar": ("so0", "so1", "sg", "sgo0", "sgo1", "sout0", "sout1")}


class Em:

    def __init__(self):
        self.streams = {e: [] for e in ENGS}
        self.counts = {k: 0 for k in SEMS}

    def op(self, eng, fn, inc=None, n=1):
        if inc is not None:
            self.counts[inc] += n
        self.streams[eng].append((fn, inc, n))
        return self.counts[inc] if inc else None

    def dma(self, eng, fn, sem):
        return self.op(eng, fn, inc=sem, n=16)

    def wait(self, eng, sem, val):
        if val and val > 0:
            self.streams[eng].append(
                (lambda e, sems, s=sem, v=val: e.wait_ge(sems[s], v), None, 0))

    def now(self, sem):
        return self.counts[sem]

    def mark(self, eng, mksem):
        for sname in QSEMS[eng]:
            self.wait(eng, sname, self.counts[sname])
        self.counts[mksem] += 16
        val = self.counts[mksem]
        self.streams[eng].append(
            (lambda e, sems, s=mksem: e.sem_inc(sems[s], 16), None, 0))
        return val

    def barrier(self):
        marks = {}
        for sname in QSEMS["scalar"]:
            self.wait("scalar", sname, self.counts[sname])
        for e in ("vector", "scalar", "tensor"):
            if e == "tensor":
                sn = "mkt"
                self.op(e, lambda en, sems: en.nop(nofuse=True), inc=sn)
            elif e == "vector":
                sn = "v"
                self.vfl()
            else:
                sn = "s"
                self.sfl()
            marks[e] = (sn, self.now(sn))
        marks["sync"] = ("mks", self.mark("sync", "mks"))
        marks["gpsimd"] = ("mkg", self.mark("gpsimd", "mkg"))
        for e in ENGS:
            for o in ENGS:
                if o != e:
                    self.wait(e, marks[o][0], marks[o][1])


def build_program(debug=False, phases=5, statdump=False, k_limit=64, skip_attn=False, attn_mode=3):
    nc = bass.Bass("TRN2", target_bir_lowering=False)

    def inp(name, shape, dt=F32):
        return nc.dram_tensor(name, list(shape), dt, kind="ExternalInput").ap()

    x_h = inp("x", (S, D))
    wk_h = inp("wk", (D, D))
    wv_h = inp("wv", (D, DHID))
    wf_h = inp("wf", (DHID, D))
    bk_h = inp("bk_t", (128, 8))
    bvbc_h = inp("bv_bc", (128, DHID))
    bfbc_h = inp("bf_bc", (128, D))
    cw1_h = inp("cn_w1", (D, D))
    cw2_h = inp("cn_w2", (D, 512))
    cw3_h = inp("cn_w3", (512, 256))
    cw4_h = inp("cn_w4t", (128, 2))
    cb1_h = inp("cn_b1t", (128, 8))
    cb2_h = inp("cn_b2t", (128, 4))
    cb3_h = inp("cn_b3t", (128, 2))
    cb4_h = inp("cn_b4s", (1, 1))
    gw1_h = inp("gc_w1", (D, 512))
    gw2_h = inp("gc_w2", (512, 256))
    gw3_h = inp("gc_w3t", (128, 2))
    gb1_h = inp("gc_b1t", (128, 4))
    gb2_h = inp("gc_b2t", (128, 2))
    gb3_h = inp("gc_b3s", (1, 1))
    ar_h = inp("ar_s", (1, 1))
    idf_h = inp("identf", (128, 128))
    tri_h = inp("trimask", (128, 128), BF16)
    onr_h = inp("onesrow", (1, 128))
    bcr2_h = inp("bcr2i", (2, 256))

    out_h = nc.dram_tensor("out", [S, D], F32, kind="ExternalOutput").ap()
    skind = "ExternalOutput" if debug else "Internal"
    xnsc = nc.dram_tensor("xn_sc", [S, D], F32, kind=skind).ap()
    gosc = nc.dram_tensor("go_sc", [S, DHID], BF16, kind=skind).ap()
    gsc = nc.dram_tensor("g_sc", [S, D], F32, kind=skind).ap()
    if statdump:
        sdmp_h = nc.dram_tensor("sdmp", [128, 256], F32, kind="ExternalOutput").ap()
    wfbf = nc.dram_tensor("wf_bf", [DHID, D], BF16, kind="Internal").ap()

    em = Em()
    em.vfl = None
    em.sfl = None
    stack = contextlib.ExitStack()
    SB = {}

    def sbuf(name, cols, dt=F32, rows=128):
        SB[name] = stack.enter_context(nc.sbuf_tensor(name, [rows, cols], dt))
        return SB[name]

    xnT = sbuf("xnT", NCD * S, BF16)
    AR1 = sbuf("AR1", 32768, BF16)
    ARF = sbuf("ARF", 11264, F32)
    ARB = sbuf("ARB", 16640, BF16)
    idf = sbuf("idf", 128)
    tri = sbuf("tri", 128, BF16)
    onr = sbuf("onr", 128, rows=1)
    bk_t = sbuf("bk", 8)
    cb1 = sbuf("cb1", 8); cb2 = sbuf("cb2", 4); cb3 = sbuf("cb3", 2)
    gb1 = sbuf("gb1", 4); gb2 = sbuf("gb2", 2)
    sc11 = sbuf("sc11", 8, rows=1)
    kon = sbuf("kon", 8)
    gsum = sbuf("gsum", 8)
    gcol = sbuf("gcol", 8)
    gch1 = sbuf("gch1", 4)
    gch2 = sbuf("gch2", 2)
    rs = sbuf("rs", 4)
    stat = sbuf("stat", 12)
    flc = sbuf("flc", 8)
    if statdump:
        sdmp = sbuf("sdmp_b", 256)

    PS_A = stack.enter_context(nc.psum_tensor("PS_A", [128, 2048], F32))
    PS_B = stack.enter_context(nc.psum_tensor("PS_B", [128, 2048], F32))
    PT_F = PS_A[:, 1024:1536]
    PS_X = PS_A[:, 1536:2048]

    def arf(off, n):
        return ARF[:, off:off + n]

    def arb(off, n):
        return ARB[:, off:off + n]

    def ar1(off, n):
        return AR1[:, off:off + n]

    bcr = ARF[0:2, 10752:11008]
    ab2 = ARF[0:2, 11008:11136]

    def ld(dst, src, sem="dc"):
        return em.dma("sync", lambda e, sems, d=dst, s=src:
                      e.dma_start(out=d, in_=s), sem)

    ld(idf[:, :], idf_h[:, :])
    ld(tri[:, :], tri_h[:, :])
    ld(onr[0:1, :], onr_h[:, :])
    ld(bk_t[:, :], bk_h[:, :])
    ld(cb1[:, :], cb1_h[:, :]); ld(cb2[:, :], cb2_h[:, :]); ld(cb3[:, :], cb3_h[:, :])
    ld(gb1[:, :], gb1_h[:, :]); ld(gb2[:, :], gb2_h[:, :])
    ld(sc11[0:1, 0:1], cb4_h[:, :])
    ld(sc11[0:1, 1:2], gb3_h[:, :])
    ld(bcr[0:2, :], bcr2_h[:, :])
    d_consts = ld(sc11[0:1, 2:3], ar_h[:, :])

    vop = lambda fn, inc=None: em.op("vector", fn, inc=inc)
    sop = lambda fn, inc=None: em.op("scalar", fn, inc=inc)
    top = lambda fn, inc=None: em.op("tensor", fn, inc=inc)
    gop = lambda fn, inc=None: em.op("gpsimd", fn, inc=inc)

    def vflush():
        return vop(lambda e, sems: e.drain(fusable=False), inc="v")

    def sflush():
        return sop(lambda e, sems: e.drain(fusable=False), inc="s")

    def vdrain():
        vop(lambda e, sems: e.drain(fusable=False))

    def sdrain():
        sop(lambda e, sems: e.drain(fusable=False))

    em.vfl = vflush
    em.sfl = sflush

    vop(lambda e, sems: e.memset(kon[:, 0:1], 1.0))
    vop(lambda e, sems: e.memset(kon[:, 1:2], 1.0 / 1024.0))
    vop(lambda e, sems: e.memset(kon[:, 2:3], 1e-5))
    vop(lambda e, sems: e.memset(kon[:, 3:4], -20.0))
    vop(lambda e, sems: e.memset(kon[:, 4:5], 1.0 / 4096.0))
    vop(lambda e, sems: e.memset(kon[:, 5:6], 0.5))
    v_kon = vop(lambda e, sems: e.memset(gsum[:, :], 0.0), inc="v")

    wkbf = arb(0, 8192)
    g_wk = em.dma("gpsimd", lambda e, sems: e.dma_start(
        out=wkbf.rearrange("p (c m) -> p c m", m=1024),
        in_=wk_h.rearrange("(c p) m -> p c m", p=128)), "gwk")
    g_wf = em.dma("gpsimd", lambda e, sems: e.dma_start(
        out=wfbf[:, :], in_=wf_h[:, :]), "gwf")

    xt4 = [arf(i * 1024, 1024) for i in range(4)]
    xn4 = [arf(4096 + i * 1024, 1024) for i in range(4)]
    KT = AR1

    for e in ("vector", "scalar", "tensor", "gpsimd"):
        em.wait(e, "dc", d_consts)
    em.wait("vector", "v", v_kon)
    em.wait("tensor", "gwk", g_wk)

    stsem = ["so0", "so1", "sgo0", "sgo1"]
    P1S = {"tr_v": 0, "tr_g": 0, "gx_free": 0, "kidx": 0}
    xt_free = [0] * 4
    nrm = [0] * 4
    xn_t = [0] * 4
    xn_d = [0] * 4
    kev = [0, 0]

    def ln_loads(g):
        dsem = "dx0" if g % 2 == 0 else "dx1"
        gld = 0
        for k in range(4):
            i = 4 * g + k
            em.wait("sync", "v", xt_free[k])
            gld = em.dma("sync", lambda e, sems, kk=k, ii=i: e.dma_start(
                out=xt4[kk][:, :], in_=x_h[ii * 128:(ii + 1) * 128, :]), dsem)
        return dsem, gld

    def ln_stats(g, dsem, gld):
        em.wait("vector", dsem, gld)
        for k in range(4):
            vop(lambda e, sems, kk=k: e.reduce_sum(
                stat[:, kk:kk + 1], xt4[kk][:, :], AX.X))
        for k in range(4):
            em.wait("vector", "t", xn_t[k])
            em.wait("vector", stsem[k], xn_d[k])
            vop(lambda e, sems, kk=k: e.tensor_tensor(
                xn4[kk][:, :], xt4[kk][:, :], xt4[kk][:, :], ALU.mult))
        vdrain()
        for k in range(4):
            vop(lambda e, sems, kk=k: e.reduce_sum(
                stat[:, 4 + kk:5 + kk], xn4[kk][:, :], AX.X))
        vdrain()
        vop(lambda e, sems: e.tensor_scalar(
            stat[:, 0:4], stat[:, 0:4], kon[:, 1:2], None, ALU.mult))
        vop(lambda e, sems: e.tensor_scalar(
            stat[:, 4:8], stat[:, 4:8], kon[:, 1:2], None, ALU.mult))
        vdrain()
        vop(lambda e, sems: e.tensor_tensor(
            stat[:, 8:12], stat[:, 0:4], stat[:, 0:4], ALU.mult))
        vdrain()
        vop(lambda e, sems: e.tensor_sub(
            stat[:, 4:8], stat[:, 4:8], stat[:, 8:12]), inc="v")
        em.wait("scalar", "v", em.now("v"))
        sop(lambda e, sems: e.activation(
            stat[:, 8:12], stat[:, 4:8], AF.Sqrt, bias=kon[:, 2:3]), inc="s")
        em.wait("vector", "s", em.now("s"))
        vop(lambda e, sems: e.reciprocal(stat[:, 4:8], stat[:, 8:12]))
        vdrain()
        for k in range(4):
            i = 4 * g + k
            vop(lambda e, sems, kk=k: e.tensor_scalar(
                xn4[kk][:, :], xt4[kk][:, :], stat[:, kk:kk + 1],
                stat[:, 4 + kk:5 + kk], ALU.subtract, ALU.mult), inc="v")
            nrm[k] = em.now("v")
            xt_free[k] = nrm[k]
            em.wait("scalar", "v", nrm[k])
            xn_d[k] = em.dma("scalar", lambda e, sems, kk=k, ii=i: e.dma_start(
                out=xnsc[ii * 128:(ii + 1) * 128, :], in_=xn4[kk][:, :]),
                stsem[k])

    def ln_tiles(g):
        for k in range(4):
            i = 4 * g + k
            em.wait("tensor", "v", nrm[k])
            em.wait("tensor", "v", P1S["gx_free"])
            for c in range(NCD):
                top(lambda e, sems, kk=k, cc=c: e.matmul(
                    PS_X[:, cc:cc + 1], xn4[kk][:, cc * 128:(cc + 1) * 128],
                    kon[:, 0:1], start=True, stop=True),
                    inc="t" if c == NCD - 1 else None)
            t_g = em.now("t")
            em.wait("tensor", "v", P1S["tr_v"])
            for c4 in range(4):
                top(lambda e, sems, kk=k, cc=c4, q=c4: e.transpose(
                    PS_B[:, q * 128:(q + 1) * 128],
                    xn4[kk][:, cc * 128:(cc + 1) * 128], idf[:, :]),
                    inc="t" if c4 == 3 else None)
            t_b0 = em.now("t")
            em.wait("tensor", "s", P1S["tr_g"])
            for c4 in range(4):
                top(lambda e, sems, kk=k, c_=4 + c4, q=c4: e.transpose(
                    PS_B[:, 512 + q * 128: 512 + (q + 1) * 128],
                    xn4[kk][:, c_ * 128:(c_ + 1) * 128], idf[:, :]),
                    inc="t" if c4 == 3 else None)
            xn_t[k] = em.now("t")
            em.wait("vector", "t", t_g)
            vop(lambda e, sems: e.tensor_add(
                gsum[:, :], gsum[:, :], PS_X[:, 0:8]), inc="v")
            P1S["gx_free"] = em.now("v")
            em.wait("vector", "t", t_b0)
            for c4 in range(4):
                vop(lambda e, sems, cc=c4, q=c4, ii=i: e.tensor_copy(
                    xnT[:, cc * S + ii * 128: cc * S + (ii + 1) * 128],
                    PS_B[:, q * 128:(q + 1) * 128]),
                    inc="v" if c4 == 3 else None)
            P1S["tr_v"] = em.now("v")
            em.wait("scalar", "t", xn_t[k])
            for c4 in range(4):
                sop(lambda e, sems, c_=4 + c4, q=c4, ii=i: e.copy(
                    xnT[:, c_ * S + ii * 128: c_ * S + (ii + 1) * 128],
                    PS_B[:, 512 + q * 128: 512 + (q + 1) * 128]),
                    inc="s" if c4 == 3 else None)
            P1S["tr_g"] = em.now("s")

    def k_block(g):
        em.wait("tensor", "v", P1S["tr_v"])
        em.wait("tensor", "s", P1S["tr_g"])
        for m in range(NCD):
            par = P1S["kidx"] % 2
            if P1S["kidx"] >= 2:
                em.wait("tensor", "v", kev[0])
            for c in range(NCD):
                top(lambda e, sems, p=par, mm=m, cc=c, tt=g: e.matmul(
                    PS_A[:, p * 512:(p + 1) * 512],
                    wkbf[:, cc * 1024 + mm * 128: cc * 1024 + (mm + 1) * 128],
                    xnT[:, cc * S + tt * 512: cc * S + (tt + 1) * 512],
                    start=(cc == 0), stop=(cc == 7)),
                    inc="t" if c == 7 else None)
            t_k = em.now("t")
            em.wait("vector", "t", t_k)
            vop(lambda e, sems, p=par, mm=m, tt=g: e.tensor_scalar(
                KT[:, mm * S + tt * 512: mm * S + (tt + 1) * 512],
                PS_A[:, p * 512:(p + 1) * 512], bk_t[:, mm:mm + 1], None,
                ALU.add), inc="v")
            kev[0], kev[1] = kev[1], em.now("v")
            P1S["kidx"] += 1

    dg = ln_loads(0)
    ln_stats(0, *dg)
    for g in range(8):
        if g + 1 < 8:
            dg = ln_loads(g + 1)
        ln_tiles(g)
        if g + 1 < 8:
            ln_stats(g + 1, *dg)
        k_block(g)

    gc1 = arf(0, 4096)
    gc2 = arf(4096, 1024)
    gc3 = arf(5120, 2)
    em.wait("sync", "v", em.now("v"))
    em.wait("sync", "t", em.now("t"))
    for k in range(4):
        em.wait("sync", stsem[k], xn_d[k])
    ld(gc1.rearrange("p (c m) -> p c m", m=512),
       gw1_h.rearrange("(c p) m -> p c m", p=128))
    ld(gc2.rearrange("p (c m) -> p c m", m=256),
       gw2_h.rearrange("(c p) m -> p c m", p=128))
    d_gc = ld(gc3, gw3_h[:, :])

    vdrain()
    vop(lambda e, sems: e.tensor_scalar(
        gcol[:, :], gsum[:, :], kon[:, 4:5], None, ALU.mult))
    v_gcol = vflush()
    em.wait("tensor", "v", v_gcol)
    em.wait("tensor", "dc", d_gc)
    g1bank = [PS_A[:, 0:1], PS_A[:, 512:513], PS_B[:, 0:1], PS_B[:, 512:513]]
    for m in range(4):
        for c in range(NCD):
            top(lambda e, sems, mm=m, cc=c: e.matmul(
                g1bank[mm], gc1[:, cc * 512 + mm * 128: cc * 512 + (mm + 1) * 128],
                gcol[:, cc:cc + 1], start=(cc == 0), stop=(cc == 7)),
                inc="t" if (m == 3 and c == 7) else None)
    t_g1 = em.now("t")
    em.wait("scalar", "t", t_g1)
    for m in range(4):
        sop(lambda e, sems, mm=m: e.activation(
            gch1[:, mm:mm + 1], g1bank[mm], AF.Gelu, bias=gb1[:, mm:mm + 1]))
    s_g1 = sflush()
    em.wait("tensor", "s", s_g1)
    g2bank = [PS_B[:, 1024:1025], PS_B[:, 1536:1537]]
    for m in range(2):
        for c in range(4):
            top(lambda e, sems, mm=m, cc=c: e.matmul(
                g2bank[mm], gc2[:, cc * 256 + mm * 128: cc * 256 + (mm + 1) * 128],
                gch1[:, cc:cc + 1], start=(cc == 0), stop=(cc == 3)),
                inc="t" if (m == 1 and c == 3) else None)
    t_g2 = em.now("t")
    em.wait("scalar", "t", t_g2)
    for m in range(2):
        sop(lambda e, sems, mm=m: e.activation(
            gch2[:, mm:mm + 1], g2bank[mm], AF.Gelu, bias=gb2[:, mm:mm + 1]))
    s_g2 = sflush()
    em.wait("tensor", "s", s_g2)
    for c in range(2):
        top(lambda e, sems, cc=c: e.matmul(
            PT_F[0:1, 0:1], gc3[:, cc:cc + 1], gch2[:, cc:cc + 1],
            start=(cc == 0), stop=(cc == 1)), inc="t" if c == 1 else None)
    t_g3 = em.now("t")
    em.wait("scalar", "t", t_g3)
    sop(lambda e, sems: e.activation(
        sc11[0:1, 3:4], PT_F[0:1, 0:1], AF.Sigmoid, bias=sc11[0:1, 1:2]))
    s_gf = sflush()
    em.wait("vector", "s", s_gf)
    ops = [
        lambda e, sems: e.memset(sc11[0:1, 6:7], 0.7),
        lambda e, sems: e.memset(sc11[0:1, 7:8], -0.5),
        lambda e, sems: e.tensor_scalar(sc11[0:1, 6:7], sc11[0:1, 3:4],
                                        sc11[0:1, 6:7], sc11[0:1, 7:8],
                                        ALU.mult, ALU.add),
        lambda e, sems: e.tensor_tensor(sc11[0:1, 6:7], sc11[0:1, 6:7],
                                        sc11[0:1, 2:3], ALU.mult),
        lambda e, sems: e.memset(sc11[0:1, 7:8], 2.0),
        lambda e, sems: e.tensor_tensor(sc11[0:1, 6:7], sc11[0:1, 6:7],
                                        sc11[0:1, 7:8], ALU.mult),
        lambda e, sems: e.memset(sc11[0:1, 7:8], 0.1),
        lambda e, sems: e.tensor_tensor(sc11[0:1, 4:5], sc11[0:1, 6:7],
                                        sc11[0:1, 7:8], ALU.add),
        lambda e, sems: e.memset(sc11[0:1, 7:8], 0.6),
        lambda e, sems: e.tensor_tensor(sc11[0:1, 5:6], sc11[0:1, 2:3],
                                        sc11[0:1, 7:8], ALU.mult),
        lambda e, sems: e.tensor_scalar(sc11[0:1, 5:6], sc11[0:1, 5:6],
                                        kon[0:1, 5:6], None, ALU.mult),
        lambda e, sems: e.tensor_tensor(sc11[0:1, 4:5], sc11[0:1, 4:5],
                                        sc11[0:1, 5:6], ALU.add),
        lambda e, sems: e.tensor_scalar(sc11[0:1, 6:7], sc11[0:1, 0:1],
                                        kon[0:1, 5:6], None, ALU.mult),
    ]
    for f in ops:
        vdrain()
        vop(f)
    vop(lambda e, sems: e.drain(fusable=False), inc="v")

    vdrain()
    vop(lambda e, sems: e.tensor_copy(stat[0:1, 9:10], sc11[0:1, 5:6]))
    vop(lambda e, sems: e.tensor_copy(stat[0:1, 10:11], sc11[0:1, 4:5]))
    vop(lambda e, sems: e.drain(fusable=False), inc="v")
    em.wait("tensor", "v", em.now("v"))
    top(lambda e, sems: e.matmul(PT_F[0:2, 0:1], stat[0:1, 9:11],
                                 kon[0:1, 0:1], start=True, stop=True),
        inc="t")
    em.wait("vector", "t", em.now("t"))
    vop(lambda e, sems: e.tensor_copy(stat[0:2, 11:12], PT_F[0:2, 0:1]))
    vdrain()
    vop(lambda e, sems: e.tensor_scalar(
        ab2[0:2, :], bcr[0:2, 0:128], stat[0:2, 11:12], None, ALU.mult))
    vop(lambda e, sems: e.drain(fusable=False), inc="v")

    if statdump:
        v_dmp = em.vfl()
        em.wait("sync", "v", v_dmp)
        ld(sdmp_h[:, :], sdmp[:, :])

    em.barrier()
    if phases <= 2:
        return finish(nc, em, stack, SB)

    VST2 = 520
    Vbuf = arb(0, 32 * VST2)
    attnb = ARF[:, :].bitcast(BF16)
    wvb = [attnb[:, 0:4096], attnb[:, 4096:8192]]
    bvb = [attnb[:, 8192:8704], attnb[:, 8704:9216]]
    Pbb = attnb[:, 9216:11264]
    gostg = [attnb[:, 11264:12288], attnb[:, 12288:13312]]

    wvb_cast = [0, 0]
    vmm_done = [0, 0]
    av_last = 0
    psa_guard = ("s", 0)
    psb_free = ("s", 0)
    go_dma = [0, 0]
    bv_done = [0, 0]
    NQB2 = 16

    for hp in range(8 if not skip_attn else 0):
        p2 = hp % 2
        em.wait("gpsimd", "t", vmm_done[p2])
        wvb_cast[p2] = em.dma("gpsimd", lambda e, sems, h_=hp, p=p2: e.dma_start(
            out=wvb[p].rearrange("p (c n) -> p c n", n=512),
            in_=wv_h[:, h_ * 512:(h_ + 1) * 512]
            .rearrange("(c p) n -> p c n", p=128)), "gv0" if p2 == 0 else "gv1")
        em.wait("gpsimd", "v", bv_done[p2])
        d_bv = em.dma("gpsimd", lambda e, sems, h_=hp, p=p2: e.dma_start(
            out=bvb[p], in_=bvbc_h[:, h_ * 512:(h_ + 1) * 512]),
            "gb0" if p2 == 0 else "gb1")
        em.wait("tensor", "gv0" if p2 == 0 else "gv1", wvb_cast[p2])
        em.wait("vector", "gb0" if p2 == 0 else "gb1", d_bv)
        em.wait("vector", "t", av_last)
        vevs = {}
        for kc in range(32):
            pb = kc % 2
            sem, val = psa_guard
            em.wait("tensor", sem, val)
            if kc >= 2:
                em.wait("tensor", "v", vevs[kc - 1])
            for c in range(NCD):
                top(lambda e, sems, p=pb, cc=c, kk=kc, w=p2: e.matmul(
                    PS_A[:, p * 512:(p + 1) * 512],
                    xnT[:, cc * S + kk * 128: cc * S + (kk + 1) * 128],
                    wvb[w][:, cc * 512:(cc + 1) * 512],
                    start=(cc == 0), stop=(cc == 7)),
                    inc="t" if c == 7 else None)
            t_v = em.now("t")
            em.wait("vector", "t", t_v)
            vop(lambda e, sems, p=pb, kk=kc, w=p2: e.tensor_tensor(
                Vbuf[:, kk * VST2: kk * VST2 + 256],
                PS_A[:, p * 512: p * 512 + 256], bvb[w][:, 0:256], ALU.add))
            vevs[kc] = vop(lambda e, sems, p=pb, kk=kc, w=p2: e.tensor_tensor(
                Vbuf[:, kk * VST2 + 260: kk * VST2 + 516],
                PS_A[:, p * 512 + 256: p * 512 + 512], bvb[w][:, 256:512],
                ALU.add), inc="v")
        vmm_done[p2] = em.now("t")
        vop(lambda e, sems: e.memset(Vbuf[:, 256::VST2], 1.0))
        vop(lambda e, sems: e.memset(Vbuf[:, 516::VST2], 1.0))
        v_ones = em.vfl()
        psa_guard = ("v", v_ones)
        bv_done[p2] = em.now("v")

        first_av_of_pair = True
        for hh in range(2):
          poff = 64 * hh
          for qb in range(NQB if attn_mode >= 2 else 0):
            n_k = 4 * qb + 4
            score_done = {}
            exp_fl = {}
            tri_done = {}
            avd = {}

            def escores(kc):
                bank = kc % 4
                if kc >= 4:
                    em.wait("tensor", "s", exp_fl[(kc - 4) // 2])
                else:
                    sem, val = psa_guard
                    em.wait("tensor", sem, val)
                top(lambda e, sems, kk=kc, b=bank, h_=hp, po=poff, q=qb:
                    e.matmul(
                    PS_A[:, b * 512:(b + 1) * 512],
                    KT[po:po + 64, h_ * S + kk * 128: h_ * S + (kk + 1) * 128],
                    xnT[po:po + 64, h_ * S + q * 512: h_ * S + (q + 1) * 512],
                    start=True, stop=True), inc="t")
                score_done[kc] = em.now("t")

            def eexp(g):
                bank0 = (2 * g) % 4
                em.wait("scalar", "t", score_done[2 * g + 1])
                if g >= 2 and (2 * g - 3) in avd:
                    em.wait("scalar", "t", avd[2 * g - 3])
                sop(lambda e, sems, b0=bank0: e.activation(
                    Pbb[:, b0 * 512: b0 * 512 + 1024],
                    PS_A[:, b0 * 512: b0 * 512 + 1024], AF.Exp,
                    bias=kon[:, 3:4]), inc="s")
                exp_fl[g] = sflush()
                for kc in (2 * g, 2 * g + 1):
                    r = kc - 4 * qb
                    if 0 <= r <= 3:
                        em.wait("vector", "s", exp_fl[g])
                        vop(lambda e, sems, b=kc % 4, rr=r: e.tensor_mul(
                            Pbb[:, b * 512 + rr * 128: b * 512 + (rr + 1) * 128],
                            Pbb[:, b * 512 + rr * 128: b * 512 + (rr + 1) * 128],
                            tri[:, :]))
                        tri_done[kc] = em.vfl()

            def eav(kc):
                nonlocal first_av_of_pair
                bank = kc % 4
                u_min = max(0, kc - 4 * qb)
                em.wait("tensor", "s", exp_fl[kc // 2])
                if kc in tri_done:
                    em.wait("tensor", "v", tri_done[kc])
                if first_av_of_pair:
                    em.wait("tensor", "v", v_ones)
                    first_av_of_pair = False
                if kc == 0:
                    sem, val = psb_free
                    em.wait("tensor", sem, val)
                for u in range(u_min, 4):
                    top(lambda e, sems, kk=kc, b=bank, uu=u, q=qb, h2=hh:
                        e.matmul(
                        PS_B[:, uu * 512: uu * 512 + 257],
                        Pbb[:, b * 512 + uu * 128: b * 512 + (uu + 1) * 128],
                        Vbuf[:, kk * VST2 + h2 * 260: kk * VST2 + h2 * 260 + 257],
                        start=(kk == 0), stop=(kk == 4 * q + uu)),
                        inc="t" if u == 3 else None)
                avd[kc] = em.now("t")

            for g in range(n_k // 2):
                escores(2 * g)
                escores(2 * g + 1)
                eexp(g)
                if attn_mode >= 3 and g >= 1:
                    eav(2 * g - 2)
                    eav(2 * g - 1)
            if attn_mode >= 3:
                eav(n_k - 2)
                eav(n_k - 1)
                av_last = avd[n_k - 1]
            psa_guard = ("s", exp_fl[n_k // 2 - 1])
            em.wait("vector", "t", av_last)
            vop(lambda e, sems: e.reciprocal(rs[:, 0:4], PS_B[:, 256::512]))
            vdrain()
            em.wait("vector", "sgo0" if qb % 2 == 0 else "sgo1", go_dma[qb % 2])
            for u in range(4):
                vop(lambda e, sems, uu=u, p=qb % 2: e.tensor_scalar(
                    gostg[p][:, uu * 256:(uu + 1) * 256],
                    PS_B[:, uu * 512: uu * 512 + 256], rs[:, uu:uu + 1],
                    None, ALU.mult))
            psb_free = ("v", vflush())
            sem_v, val_v = psb_free
            em.wait("scalar", sem_v, val_v)
            go_dma[qb % 2] = em.dma(
                "scalar", lambda e, sems, q=qb, h_=2 * hp + hh, p=qb % 2:
                e.dma_start(
                    out=gosc[q * 512:(q + 1) * 512, h_ * 256:(h_ + 1) * 256]
                    .rearrange("(u p) n -> p u n", p=128),
                    in_=gostg[p][:, :].rearrange("p (u n) -> p u n", n=256)),
                "sgo0" if qb % 2 == 0 else "sgo1")

    em.barrier()
    if phases <= 3:
        return finish(nc, em, stack, SB)

    gbuf = None
    ggrp_dma = [[0] * 4, [0] * 4]
    ggel = [[0] * 4, [0] * 4]
    goT_free = [0, 0]

    def ggrp_sem(g):
        return "sh0" if g % 2 == 0 else "sh1"

    def emit_goT_dmas(qb):
        b = qb % 2
        em.wait("sync", "t", goT_free[b])
        for g in range(4):
            for c in range(g * 8, g * 8 + 8):
                ggrp_dma[b][g] = em.dma(
                    "sync", lambda e, sems, q=qb, cc=c, bb=b: e.dma_start(
                        out=gbuf[bb][:, cc * 512:(cc + 1) * 512],
                        in_=gosc[q * 512:(q + 1) * 512, cc * 128:(cc + 1) * 128],
                        transpose=True), ggrp_sem(g))

    def emit_goT_gelus(qb, groups):
        b = qb % 2
        for g in groups:
            em.wait("scalar", ggrp_sem(g), ggrp_dma[b][g])
            g_ = 0
            for c in range(g * 8, g * 8 + 8):
                g_ = sop(lambda e, sems, cc=c, bb=b: e.activation(
                    gbuf[bb][:, cc * 512:(cc + 1) * 512],
                    gbuf[bb][:, cc * 512:(cc + 1) * 512], AF.Gelu), inc="s")
            ggel[b][g] = g_

    cw1b = ar1(0, 8192)
    cw2b = ar1(8192, 12288 - 8192)
    cw3b = ar1(12288, 1024)
    cw4b = ar1(13312, 2)
    h1b = ar1(13440, 2048)
    h2b = ar1(15488, 1024)
    h3b = ar1(16512, 512)
    outcb = ar1(17024, 2048)

    xnf = [arf(0, 2048), arf(2048, 2048)]
    gT = arf(8192, 2048)
    gnat = arf(10240, 512)
    twtmp8 = AR1[:, 19072:23168].bitcast(F32)

    g_c1 = em.dma("gpsimd", lambda e, sems: e.dma_start(
        out=cw1b.rearrange("p (c m) -> p c m", m=1024),
        in_=cw1_h.rearrange("(c p) m -> p c m", p=128)), "gcn")
    g_c2 = em.dma("gpsimd", lambda e, sems: e.dma_start(
        out=cw2b.rearrange("p (c m) -> p c m", m=512),
        in_=cw2_h.rearrange("(c p) m -> p c m", p=128)), "gcn")
    g_c3 = em.dma("gpsimd", lambda e, sems: e.dma_start(
        out=cw3b.rearrange("p (c m) -> p c m", m=256),
        in_=cw3_h.rearrange("(c p) m -> p c m", p=128)), "gcn")
    g_c4 = em.dma("gpsimd", lambda e, sems: e.dma_start(
        out=cw4b, in_=cw4_h[:, :]), "gcn")
    em.wait("tensor", "gcn", g_c4)
    gbuf = [arb(0, 16384), xnT[:, 0:16384]]
    emit_goT_dmas(0)

    sgn = {0: (4, 1.0), 1: (5, 1.0), 2: (6, -1.0), 3: (7, -1.0),
           4: (0, -1.0), 5: (1, -1.0), 6: (2, 1.0), 7: (3, 1.0)}
    xnf_dma = [0, 0]
    xnf_free = [0, 0]
    tr_ev = [0, 0]
    hg = [0, 0]
    gb = [0]
    G = {"last_sig": 0, "last_bcast": 0, "ptf_free": 0, "outcb_free": 0,
         "gT_free": 0, "gnat_dma": 0}
    sA = {"outT": [arf(4096, 2048), arf(6144, 2048)], "buf": 0, "xp": 0,
          "dsem": "dx0", "fq": 0, "cast": 0, "sig": 0, "bcast": 0, "twist": 0,
          "tfT": 0}
    sB = {"outT": [AR1[:, 23168:27264].bitcast(F32),
                   AR1[:, 27264:31360].bitcast(F32)], "buf": 0, "xp": 1,
          "dsem": "dx1", "fq": 1, "cast": 0, "sig": 0, "bcast": 0, "twist": 0,
          "tfT": 0}

    def mm_group(lhsbuf, lstep, m_, ncc, rhs_fn, evdst, evoff, bias):
        b = gb[0] % 2
        gb[0] += 1
        em.wait("tensor", "s", hg[0])
        for c in range(ncc):
            l_ap = lhsbuf[:, c * lstep + m_ * 128: c * lstep + (m_ + 1) * 128]
            r_ap = rhs_fn(c)
            top(lambda e, sems, o=PS_A[:, b * 512: b * 512 + 256], l=l_ap,
                r=r_ap, s0=(c == 0), s1=(c == ncc - 1): e.matmul(
                o, l, r, start=s0, stop=s1),
                inc="t" if c == ncc - 1 else None)
        em.wait("scalar", "t", em.now("t"))
        ev = sop(lambda e, sems, o=evdst[:, evoff:evoff + 256],
                 i=PS_A[:, b * 512: b * 512 + 256], bi=bias: e.activation(
                 o, i, AF.Gelu, bias=bi), inc="s")
        hg[0], hg[1] = hg[1], ev

    def mm_h1(st, cyc):
        if cyc == 0:
            f = st["fq"]
            rhs = lambda c, f_=f: xnT[:, c * S + f_ * 256: c * S + (f_ + 1) * 256]
        else:
            em.wait("tensor", "v", st["cast"])
            rhs = lambda c: outcb[:, c * 256:(c + 1) * 256]
        for m_ in range(8):
            mm_group(cw1b, 1024, m_, 8, rhs, h1b, m_ * 256, cb1[:, m_:m_ + 1])
        G["outcb_free"] = em.now("t")

    def mm_rest(st):
        em.wait("tensor", "s", hg[1])
        for m_ in range(4):
            mm_group(cw2b, 512, m_, 8,
                     lambda c: h1b[:, c * 256:(c + 1) * 256],
                     h2b, m_ * 256, cb2[:, m_:m_ + 1])
        em.wait("tensor", "s", hg[1])
        for m_ in range(2):
            mm_group(cw3b, 256, m_, 4,
                     lambda c: h2b[:, c * 256:(c + 1) * 256],
                     h3b, m_ * 256, cb3[:, m_:m_ + 1])
        em.wait("tensor", "s", hg[1])
        em.wait("tensor", "s", G["last_sig"])
        for c in range(2):
            top(lambda e, sems, cc=c: e.matmul(
                PS_X[0:1, 0:256], cw4b[:, cc:cc + 1],
                h3b[:, cc * 256:(cc + 1) * 256],
                start=(cc == 0), stop=(cc == 1)), inc="t" if c == 1 else None)
        st["tfT"] = em.now("t")

    def tail_pre(st):
        em.wait("scalar", "t", st["tfT"])
        em.wait("scalar", "t", G["last_bcast"])
        st["sig"] = sop(lambda e, sems: e.activation(
            bcr[0:1, :], PS_X[0:1, 0:256], AF.Tanh,
            bias=sc11[0:1, 6:7], scale=kon[0:1, 5:6]), inc="s")
        G["last_sig"] = st["sig"]

    def tail_bcast(st):
        em.wait("tensor", "s", st["sig"])
        em.wait("tensor", "v", G["ptf_free"])
        top(lambda e, sems: e.matmul(PT_F[:, 0:256], ab2[0:2, :], bcr[0:2, :],
                                     start=True, stop=True), inc="t")
        st["bcast"] = em.now("t")
        G["last_bcast"] = st["bcast"]

    def tail_post(st, cyc):
        cur, nxt = st["buf"], 1 - st["buf"]
        oT = st["outT"]
        em.wait("vector", "t", st["bcast"])
        for j in range(8):
            srcc = sgn[j][0]
            vop(lambda e, sems, cu=cur, ss=srcc, jj=j, o=oT: e.tensor_tensor(
                twtmp8[:, jj * 256:(jj + 1) * 256], PT_F[:, 0:256],
                o[cu][:, ss * 256:(ss + 1) * 256], ALU.mult),
                inc="v" if j == 7 else None)
        G["ptf_free"] = em.now("v")
        vdrain()
        for j in range(8):
            sg_ = sgn[j][1]
            vop(lambda e, sems, cu=cur, nx=nxt, jj=j, g_=sg_, o=oT:
                e.tensor_tensor(o[nx][:, jj * 256:(jj + 1) * 256],
                                o[cu][:, jj * 256:(jj + 1) * 256],
                                twtmp8[:, jj * 256:(jj + 1) * 256],
                                ALU.add if g_ > 0 else ALU.subtract),
                inc="v" if (cyc == 2 and j == 7) else None)
        if cyc < 2:
            vdrain()
            em.wait("vector", "t", G["outcb_free"])
            vop(lambda e, sems, nx=nxt, o=oT: e.tensor_copy(
                outcb[:, :], o[nx][:, :]), inc="v")
            st["cast"] = em.now("v")
        else:
            st["twist"] = em.now("v")
        st["buf"] = nxt

    def gend(st):
        fq, cur, oT = st["fq"], st["buf"], st["outT"]
        em.wait("scalar", "v", st["twist"])
        em.wait("scalar", "t", G["gT_free"])
        s0 = sop(lambda e, sems, cu=cur, o=oT: e.activation(
            gT[:, :], o[cu][:, :], AF.Tanh, scale=kon[:, 5:6]), inc="s")
        em.wait("vector", "s", s0)
        vop(lambda e, sems: e.tensor_scalar(
            gT[:, :], gT[:, :], kon[:, 5:6], kon[:, 5:6], ALU.mult, ALU.add))
        vdrain()
        vop(lambda e, sems, cu=cur, o=oT: e.tensor_tensor(
            gT[:, :], o[cu][:, :], gT[:, :], ALU.mult), inc="v")
        em.wait("tensor", "v", em.now("v"))
        for s2 in range(2):
            for rnd in range(2):
                pb = rnd % 2
                bb = pb * 512
                em.wait("tensor", "v", tr_ev[pb])
                for q4 in range(4):
                    c = rnd * 4 + q4
                    top(lambda e, sems, ss=s2, cc=c, q=q4, b=bb: e.transpose(
                        PS_B[:, b + q * 128: b + (q + 1) * 128],
                        gT[:, cc * 256 + ss * 128: cc * 256 + ss * 128 + 128],
                        idf[:, :]), inc="t" if q4 == 3 else None)
                em.wait("vector", "t", em.now("t"))
                em.wait("vector", "sg", G["gnat_dma"])
                for q4 in range(4):
                    vop(lambda e, sems, q=q4, b=bb: e.tensor_copy(
                        gnat[:, q * 128:(q + 1) * 128],
                        PS_B[:, b + q * 128: b + (q + 1) * 128]),
                        inc="v" if q4 == 3 else None)
                tr_ev[pb] = em.now("v")
                em.wait("scalar", "v", tr_ev[pb])
                G["gnat_dma"] = em.dma(
                    "scalar", lambda e, sems, f=fq, ss=s2, r=rnd: e.dma_start(
                        out=gsc[f * 256 + ss * 128: f * 256 + (ss + 1) * 128,
                                r * 512:(r + 1) * 512],
                        in_=gnat[:, :]), "sg")
        G["gT_free"] = em.now("t")

    def out0_build2(st, fqn):
        p = st["xp"]
        dst = st["outT"][1 - st["buf"]]
        em.wait("sync", "t", xnf_free[p])
        xnf_dma[p] = em.dma("sync", lambda e, sems, f=fqn, pp=p: e.dma_start(
            out=xnf[pp][:, :].rearrange("p (s n) -> p s n", n=1024),
            in_=xnsc[f * 256:(f + 1) * 256, :].rearrange("(s p) n -> p s n",
                                                         p=128)), st["dsem"])
        em.wait("tensor", st["dsem"], xnf_dma[p])
        for rnd in range(4):
            pb = rnd % 2
            bb = pb * 512
            em.wait("tensor", "v", tr_ev[pb])
            for q4 in range(4):
                k = rnd * 4 + q4
                s2, c = k // 8, k % 8
                top(lambda e, sems, pp=p, ss=s2, cc=c, q=q4, b=bb: e.transpose(
                    PS_B[:, b + q * 128: b + (q + 1) * 128],
                    xnf[pp][:, ss * 1024 + cc * 128: ss * 1024 + (cc + 1) * 128],
                    idf[:, :]), inc="t" if q4 == 3 else None)
            em.wait("vector", "t", em.now("t"))
            for q4 in range(4):
                k = rnd * 4 + q4
                s2, c = k // 8, k % 8
                vop(lambda e, sems, ss=s2, cc=c, q=q4, b=bb, d=dst:
                    e.tensor_copy(
                    d[:, cc * 256 + ss * 128: cc * 256 + ss * 128 + 128],
                    PS_B[:, b + q * 128: b + (q + 1) * 128]),
                    inc="v" if q4 == 3 else None)
            tr_ev[pb] = em.now("v")
        xnf_free[p] = em.now("t")
        st["fq"] = fqn
        st["buf"] = 1 - st["buf"]

    out0_build2(sA, 0)
    out0_build2(sB, 1)
    for ss2 in range(NFQ // 2):
        for cyc in range(3):
            if cyc == 0:
                mm_h1(sA, 0); mm_rest(sA); tail_pre(sA)
            mm_h1(sB, cyc)
            tail_bcast(sA); tail_post(sA, cyc)
            mm_rest(sB); tail_pre(sB)
            if cyc < 2:
                mm_h1(sA, cyc + 1)
                tail_bcast(sB); tail_post(sB, cyc)
                mm_rest(sA); tail_pre(sA)
            else:
                gend(sA)
                tail_bcast(sB); tail_post(sB, 2)
                if ss2 + 1 < NFQ // 2:
                    out0_build2(sA, 2 * ss2 + 2)
                gend(sB)
                if ss2 + 1 < NFQ // 2:
                    out0_build2(sB, 2 * ss2 + 3)
        if ss2 == 5:
            emit_goT_gelus(0, (0, 1))
        if ss2 == 6:
            emit_goT_gelus(0, (2, 3))

    em.barrier()
    if phases <= 4:
        return finish(nc, em, stack, SB)

    sstg4 = arf(0, 4096)
    xnti4 = arf(4096, 4096)
    bfbc = arf(8192, 1024)
    tl4 = xnT[:, 16384:32768].bitcast(F32)
    gti4 = tl4[:, 0:4096]
    xti4 = tl4[:, 4096:8192]
    d_bf = em.dma("sync", lambda e, sems: e.dma_start(
        out=bfbc, in_=bfbc_h[:, :]), "dc")
    em.wait("vector", "dc", d_bf)

    bank8 = [PS_A[:, 0:512], PS_A[:, 512:1024],
             PS_B[:, 0:512], PS_B[:, 512:1024],
             PS_B[:, 1024:1536], PS_B[:, 1536:2048],
             PT_F[:, 0:512], PS_X[:, 0:512]]

    em.wait("gpsimd", "gwf", g_wf)
    wfgrp = [0] * 4
    for g4 in range(4):
        for c in range(g4 * 8, g4 * 8 + 8):
            wfgrp[g4] = em.dma("gpsimd", lambda e, sems, cc=c: e.dma_start(
                out=AR1[:, cc * 1024:(cc + 1) * 1024],
                in_=wfbf[cc * 128:(cc + 1) * 128, :]),
                "gv0" if g4 % 2 == 0 else "gv1")

    tile_free = [0, 0, 0, 0]
    usem = ["dt0", "dt1", "dx0", "dx1"]
    ev_done_qb = 0
    out_dma = [0, 0]

    def load_tiles(qb, u):
        em.wait("sync", "v", tile_free[u])
        r0 = qb * 512 + u * 128
        em.dma("sync", lambda e, sems, r=r0, uu=u: e.dma_start(
            out=gti4[:, uu * 1024:(uu + 1) * 1024], in_=gsc[r:r + 128, :]),
            usem[u])
        em.dma("sync", lambda e, sems, r=r0, uu=u: e.dma_start(
            out=xti4[:, uu * 1024:(uu + 1) * 1024], in_=x_h[r:r + 128, :]),
            usem[u])
        return em.dma("sync", lambda e, sems, r=r0, uu=u: e.dma_start(
            out=xnti4[:, uu * 1024:(uu + 1) * 1024], in_=xnsc[r:r + 128, :]),
            usem[u])

    for qb in range(NQB):
        b = qb % 2
        tile_dma = [load_tiles(qb, u) for u in range(4)]
        if qb + 1 < NQB:
            emit_goT_dmas(qb + 1)
        gel = list(ggel[b])
        em.wait("tensor", "v", ev_done_qb)
        for c in range(32):
            if qb == 0 and c % 8 == 0:
                em.wait("tensor", "gv0" if (c // 8) % 2 == 0 else "gv1",
                        wfgrp[c // 8])
            if c % 8 == 0:
                em.wait("tensor", "s", gel[c // 8])
            for u in range(4):
                for dh in range(2):
                    top(lambda e, sems, cc=c, uu=u, d_=dh, bb=b: e.matmul(
                        bank8[uu * 2 + d_],
                        gbuf[bb][:, cc * 512 + uu * 128: cc * 512 + (uu + 1) * 128],
                        AR1[:, cc * 1024 + d_ * 512: cc * 1024 + (d_ + 1) * 512],
                        start=(cc == 0), stop=(cc == 31)),
                        inc="t" if (u == 3 and dh == 1) else None)
        goT_free[b] = em.now("t")
        smm = em.now("t")
        em.wait("vector", "t", smm)
        em.wait("vector", "sout0", out_dma[0])
        em.wait("vector", "sout1", out_dma[1])
        for u in range(4):
            vop(lambda e, sems, uu=u: e.tensor_add(
                sstg4[:, uu * 1024: uu * 1024 + 512],
                bank8[uu * 2], bfbc[:, 0:512]))
            vop(lambda e, sems, uu=u: e.tensor_add(
                sstg4[:, uu * 1024 + 512: uu * 1024 + 1024],
                bank8[uu * 2 + 1], bfbc[:, 512:1024]))
        ev_done_qb = vflush()
        if qb + 1 < NQB:
            emit_goT_gelus(qb + 1, (0, 1, 2, 3))
        for u in range(4):
            em.wait("vector", usem[u], tile_dma[u])
            st = sstg4[:, u * 1024:(u + 1) * 1024]
            vop(lambda e, sems, uu=u, t_=st: e.tensor_add(
                t_, t_, xnti4[:, uu * 1024:(uu + 1) * 1024]))
            vdrain()
            vop(lambda e, sems, uu=u, t_=st: e.tensor_mul(
                t_, t_, gti4[:, uu * 1024:(uu + 1) * 1024]))
            vdrain()
            vop(lambda e, sems, uu=u, t_=st: e.tensor_add(
                t_, t_, xti4[:, uu * 1024:(uu + 1) * 1024]))
            v_out = vflush()
            tile_free[u] = v_out
            em.wait("scalar", "v", v_out)
            out_dma[u % 2] = em.dma("scalar", lambda e, sems, q=qb, uu=u,
                                    t_=st: e.dma_start(
                out=out_h[q * 512 + uu * 128: q * 512 + (uu + 1) * 128, :],
                in_=t_), "sout0" if u % 2 == 0 else "sout1")

    em.wait("scalar", "sout0", em.now("sout0"))
    em.wait("scalar", "sout1", em.now("sout1"))
    return finish(nc, em, stack, SB)


def finish(nc, em, stack, SB):
    sems = {}
    semstack = contextlib.ExitStack()
    for sname in SEMS:
        sems[sname] = semstack.enter_context(nc.semaphore(sname))
    with nc.Block() as block:
        def run(engobj, stream):
            for fn, inc, n in stream:
                instr = fn(engobj, sems)
                if inc is not None and instr is not None:
                    instr.then_inc(sems[inc], n)

        @block.sync
        def _(e):
            run(e, em.streams["sync"])

        @block.gpsimd
        def _(e):
            run(e, em.streams["gpsimd"])

        @block.vector
        def _(e):
            run(e, em.streams["vector"])

        @block.scalar
        def _(e):
            run(e, em.streams["scalar"])

        @block.tensor
        def _(e):
            run(e, em.streams["tensor"])
    semstack.close()
    stack.close()
    return nc



def _host_consts():
    identf = np.eye(128, dtype=np.float32)
    trimask = np.triu(np.ones((128, 128), np.float32)).astype(ml_dtypes.bfloat16)
    onesrow = np.ones((1, 128), np.float32)
    return identf, trimask, onesrow


def _prep(inputs):
    identf, trimask, onesrow = _host_consts()
    f = np.asarray
    com = {
        "wk": np.ascontiguousarray(f(inputs["L0_attn_Wk"], np.float32)),
        "wv": np.ascontiguousarray(f(inputs["L0_attn_Wv"], np.float32)),
        "wf": np.ascontiguousarray(f(inputs["L0_attn_Wf"], np.float32)),
        "bk_t": np.ascontiguousarray(
            f(inputs["L0_attn_bk"], np.float32).reshape(8, 128).T),
        "bv_bc": np.ascontiguousarray(np.broadcast_to(
            f(inputs["L0_attn_bv"], np.float32), (128, DHID))),
        "bf_bc": np.ascontiguousarray(np.broadcast_to(
            f(inputs["L0_attn_bf"], np.float32), (128, D))),
        "cn_w1": np.ascontiguousarray(f(inputs["L0_ffn_cn_w1"], np.float32)),
        "cn_w2": np.ascontiguousarray(f(inputs["L0_ffn_cn_w2"], np.float32)),
        "cn_w3": np.ascontiguousarray(f(inputs["L0_ffn_cn_w3"], np.float32)),
        "cn_w4t": np.ascontiguousarray(
            f(inputs["L0_ffn_cn_w4"], np.float32).reshape(2, 128).T),
        "cn_b1t": np.ascontiguousarray(
            f(inputs["L0_ffn_cn_b1"], np.float32).reshape(8, 128).T),
        "cn_b2t": np.ascontiguousarray(
            f(inputs["L0_ffn_cn_b2"], np.float32).reshape(4, 128).T),
        "cn_b3t": np.ascontiguousarray(
            f(inputs["L0_ffn_cn_b3"], np.float32).reshape(2, 128).T),
        "cn_b4s": f(inputs["L0_ffn_cn_b4"], np.float32).reshape(1, 1),
        "gc_w1": np.ascontiguousarray(f(inputs["L0_ffn_gc_w1"], np.float32)),
        "gc_w2": np.ascontiguousarray(f(inputs["L0_ffn_gc_w2"], np.float32)),
        "gc_w3t": np.ascontiguousarray(
            f(inputs["L0_ffn_gc_w3"], np.float32).reshape(2, 128).T),
        "gc_b1t": np.ascontiguousarray(
            f(inputs["L0_ffn_gc_b1"], np.float32).reshape(4, 128).T),
        "gc_b2t": np.ascontiguousarray(
            f(inputs["L0_ffn_gc_b2"], np.float32).reshape(2, 128).T),
        "gc_b3s": f(inputs["L0_ffn_gc_b3"], np.float32).reshape(1, 1),
        "ar_s": f(inputs["L0_ffn_adaptive_range"], np.float32).reshape(1, 1),
        "identf": identf,
        "trimask": trimask,
        "onesrow": onesrow,
        "bcr2i": np.ones((2, 256), np.float32),
    }
    x = f(inputs["L0_attn_x"], np.float32)
    maps = []
    for b in range(8):
        m = dict(com)
        m["x"] = np.ascontiguousarray(x[b])
        maps.append(m)
    return maps


_CACHE = {}


def kernel(**inputs):
    key = "prog"
    if key not in _CACHE:
        _CACHE[key] = build_program(debug=False, phases=5)
    nc = _CACHE[key]
    maps = _prep(inputs)
    res = run_bass_kernel_spmd(nc, maps, core_ids=list(range(8)))
    return np.stack([res.results[b]["out"] for b in range(8)], axis=0)


# version 22
# speedup vs baseline: 1.0064x; 1.0064x over previous
import contextlib
import numpy as np
import ml_dtypes
import concourse.bass as bass
import concourse.mybir as mybir
from concourse.bass_utils import run_bass_kernel_spmd

F32 = mybir.dt.float32
BF16 = mybir.dt.bfloat16
AF = mybir.ActivationFunctionType
ALU = mybir.AluOpType
AX = mybir.AxisListType

S, D, H, DH, DV, DHID = 4096, 1024, 16, 64, 256, 4096
NT, NQB, NFQ, NCD = 32, 8, 16, 8
VST = 260

ENGS = ("sync", "gpsimd", "vector", "scalar", "tensor")
SEMS = ("dc", "dx0", "dx1", "dt0", "dt1", "mks",
        "gwk", "gwf", "gcn", "gv0", "gv1", "gb0", "gb1", "mkg",
        "so0", "so1", "sg", "sgo0", "sgo1", "sh0", "sh1", "sout0", "sout1",
        "t", "v", "s", "g", "mkt")
QSEMS = {"sync": ("dc", "dx0", "dx1", "dt0", "dt1", "sh0", "sh1"),
         "gpsimd": ("gwk", "gwf", "gcn", "gv0", "gv1", "gb0", "gb1"),
         "scalar": ("so0", "so1", "sg", "sgo0", "sgo1", "sout0", "sout1")}


class Em:

    def __init__(self):
        self.streams = {e: [] for e in ENGS}
        self.counts = {k: 0 for k in SEMS}

    def op(self, eng, fn, inc=None, n=1):
        if inc is not None:
            self.counts[inc] += n
        self.streams[eng].append((fn, inc, n))
        return self.counts[inc] if inc else None

    def dma(self, eng, fn, sem):
        return self.op(eng, fn, inc=sem, n=16)

    def wait(self, eng, sem, val):
        if val and val > 0:
            self.streams[eng].append(
                (lambda e, sems, s=sem, v=val: e.wait_ge(sems[s], v), None, 0))

    def now(self, sem):
        return self.counts[sem]

    def mark(self, eng, mksem):
        for sname in QSEMS[eng]:
            self.wait(eng, sname, self.counts[sname])
        self.counts[mksem] += 16
        val = self.counts[mksem]
        self.streams[eng].append(
            (lambda e, sems, s=mksem: e.sem_inc(sems[s], 16), None, 0))
        return val

    def barrier(self):
        marks = {}
        for sname in QSEMS["scalar"]:
            self.wait("scalar", sname, self.counts[sname])
        for e in ("vector", "scalar", "tensor"):
            if e == "tensor":
                sn = "mkt"
                self.op(e, lambda en, sems: en.nop(nofuse=True), inc=sn)
            elif e == "vector":
                sn = "v"
                self.vfl()
            else:
                sn = "s"
                self.sfl()
            marks[e] = (sn, self.now(sn))
        marks["sync"] = ("mks", self.mark("sync", "mks"))
        marks["gpsimd"] = ("mkg", self.mark("gpsimd", "mkg"))
        for e in ENGS:
            for o in ENGS:
                if o != e:
                    self.wait(e, marks[o][0], marks[o][1])


def build_program(debug=False, phases=5, statdump=False, k_limit=64, skip_attn=False, attn_mode=3):
    nc = bass.Bass("TRN2", target_bir_lowering=False)

    def inp(name, shape, dt=F32):
        return nc.dram_tensor(name, list(shape), dt, kind="ExternalInput").ap()

    x_h = inp("x", (S, D))
    wk_h = inp("wk", (D, D))
    wv_h = inp("wv", (D, DHID))
    wf_h = inp("wf", (DHID, D))
    bk_h = inp("bk_t", (128, 8))
    bvbc_h = inp("bv_bc", (128, DHID))
    bfbc_h = inp("bf_bc", (128, D))
    cw1_h = inp("cn_w1", (D, D))
    cw2_h = inp("cn_w2", (D, 512))
    cw3_h = inp("cn_w3", (512, 256))
    cw4_h = inp("cn_w4t", (128, 2))
    cb1_h = inp("cn_b1t", (128, 8))
    cb2_h = inp("cn_b2t", (128, 4))
    cb3_h = inp("cn_b3t", (128, 2))
    cb4_h = inp("cn_b4s", (1, 1))
    gw1_h = inp("gc_w1", (D, 512))
    gw2_h = inp("gc_w2", (512, 256))
    gw3_h = inp("gc_w3t", (128, 2))
    gb1_h = inp("gc_b1t", (128, 4))
    gb2_h = inp("gc_b2t", (128, 2))
    gb3_h = inp("gc_b3s", (1, 1))
    ar_h = inp("ar_s", (1, 1))
    idf_h = inp("identf", (128, 128))
    tri_h = inp("trimask", (128, 128), BF16)
    onr_h = inp("onesrow", (1, 128))
    bcr2_h = inp("bcr2i", (2, 256))

    out_h = nc.dram_tensor("out", [S, D], F32, kind="ExternalOutput").ap()
    skind = "ExternalOutput" if debug else "Internal"
    xnsc = nc.dram_tensor("xn_sc", [S, D], F32, kind=skind).ap()
    gosc = nc.dram_tensor("go_sc", [S, DHID], BF16, kind=skind).ap()
    gsc = nc.dram_tensor("g_sc", [S, D], F32, kind=skind).ap()
    if statdump:
        sdmp_h = nc.dram_tensor("sdmp", [128, 256], F32, kind="ExternalOutput").ap()
    wfbf = nc.dram_tensor("wf_bf", [DHID, D], BF16, kind="Internal").ap()

    em = Em()
    em.vfl = None
    em.sfl = None
    stack = contextlib.ExitStack()
    SB = {}

    def sbuf(name, cols, dt=F32, rows=128):
        SB[name] = stack.enter_context(nc.sbuf_tensor(name, [rows, cols], dt))
        return SB[name]

    xnT = sbuf("xnT", NCD * S, BF16)
    AR1 = sbuf("AR1", 32768, BF16)
    ARF = sbuf("ARF", 11264, F32)
    ARB = sbuf("ARB", 16640, BF16)
    idf = sbuf("idf", 128)
    tri = sbuf("tri", 128, BF16)
    idb = sbuf("idb", 128, BF16)
    ngt = sbuf("ngt", 128, BF16)
    onr = sbuf("onr", 128, rows=1)
    bk_t = sbuf("bk", 8)
    cb1 = sbuf("cb1", 8); cb2 = sbuf("cb2", 4); cb3 = sbuf("cb3", 2)
    gb1 = sbuf("gb1", 4); gb2 = sbuf("gb2", 2)
    sc11 = sbuf("sc11", 8, rows=1)
    kon = sbuf("kon", 8)
    gsum = sbuf("gsum", 8)
    gcol = sbuf("gcol", 8)
    gch1 = sbuf("gch1", 4)
    gch2 = sbuf("gch2", 2)
    rs = sbuf("rs", 4)
    stat = sbuf("stat", 12)
    flc = sbuf("flc", 8)
    if statdump:
        sdmp = sbuf("sdmp_b", 256)

    PS_A = stack.enter_context(nc.psum_tensor("PS_A", [128, 2048], F32))
    PS_B = stack.enter_context(nc.psum_tensor("PS_B", [128, 2048], F32))
    PT_F = PS_A[:, 1024:1536]
    PS_X = PS_A[:, 1536:2048]

    def arf(off, n):
        return ARF[:, off:off + n]

    def arb(off, n):
        return ARB[:, off:off + n]

    def ar1(off, n):
        return AR1[:, off:off + n]

    bcr = ARF[0:2, 10752:11008]
    ab2 = ARF[0:2, 11008:11136]

    def ld(dst, src, sem="dc"):
        return em.dma("sync", lambda e, sems, d=dst, s=src:
                      e.dma_start(out=d, in_=s), sem)

    ld(idf[:, :], idf_h[:, :])
    ld(tri[:, :], tri_h[:, :])
    ld(onr[0:1, :], onr_h[:, :])
    ld(bk_t[:, :], bk_h[:, :])
    ld(cb1[:, :], cb1_h[:, :]); ld(cb2[:, :], cb2_h[:, :]); ld(cb3[:, :], cb3_h[:, :])
    ld(gb1[:, :], gb1_h[:, :]); ld(gb2[:, :], gb2_h[:, :])
    ld(sc11[0:1, 0:1], cb4_h[:, :])
    ld(sc11[0:1, 1:2], gb3_h[:, :])
    ld(bcr[0:2, :], bcr2_h[:, :])
    d_consts = ld(sc11[0:1, 2:3], ar_h[:, :])

    vop = lambda fn, inc=None: em.op("vector", fn, inc=inc)
    sop = lambda fn, inc=None: em.op("scalar", fn, inc=inc)
    top = lambda fn, inc=None: em.op("tensor", fn, inc=inc)
    gop = lambda fn, inc=None: em.op("gpsimd", fn, inc=inc)

    def vflush():
        return vop(lambda e, sems: e.drain(fusable=False), inc="v")

    def sflush():
        return sop(lambda e, sems: e.drain(fusable=False), inc="s")

    def vdrain():
        vop(lambda e, sems: e.drain(fusable=False))

    def sdrain():
        sop(lambda e, sems: e.drain(fusable=False))

    em.vfl = vflush
    em.sfl = sflush

    vop(lambda e, sems: e.memset(kon[:, 0:1], 1.0))
    vop(lambda e, sems: e.memset(kon[:, 1:2], 1.0 / 1024.0))
    vop(lambda e, sems: e.memset(kon[:, 2:3], 1e-5))
    vop(lambda e, sems: e.memset(kon[:, 3:4], -20.0))
    vop(lambda e, sems: e.memset(kon[:, 4:5], 1.0 / 4096.0))
    vop(lambda e, sems: e.memset(kon[:, 5:6], 0.5))
    vop(lambda e, sems: e.memset(kon[:, 6:7], 1e9))
    v_kon = vop(lambda e, sems: e.memset(gsum[:, :], 0.0), inc="v")

    wkbf = arb(0, 8192)
    g_wk = em.dma("gpsimd", lambda e, sems: e.dma_start(
        out=wkbf.rearrange("p (c m) -> p c m", m=1024),
        in_=wk_h.rearrange("(c p) m -> p c m", p=128)), "gwk")
    g_wf = em.dma("gpsimd", lambda e, sems: e.dma_start(
        out=wfbf[:, :], in_=wf_h[:, :]), "gwf")

    xt4 = [arf(i * 1024, 1024) for i in range(4)]
    xn4 = [arf(4096 + i * 1024, 1024) for i in range(4)]
    KT = AR1

    for e in ("vector", "scalar", "tensor", "gpsimd"):
        em.wait(e, "dc", d_consts)
    em.wait("vector", "v", v_kon)
    em.wait("tensor", "gwk", g_wk)
    vop(lambda e, sems: e.tensor_copy(idb[:, :], idf[:, :]))
    vop(lambda e, sems: e.tensor_scalar(
        ngt[:, :], tri[:, :], kon[:, 0:1], kon[:, 6:7],
        ALU.subtract, ALU.mult))

    stsem = ["so0", "so1", "sgo0", "sgo1"]
    P1S = {"tr_v": 0, "tr_g": 0, "gx_free": 0, "kidx": 0}
    xt_free = [0] * 4
    nrm = [0] * 4
    xn_t = [0] * 4
    xn_d = [0] * 4
    kev = [0, 0]

    def ln_loads(g):
        dsem = "dx0" if g % 2 == 0 else "dx1"
        gld = 0
        for k in range(4):
            i = 4 * g + k
            em.wait("sync", "v", xt_free[k])
            gld = em.dma("sync", lambda e, sems, kk=k, ii=i: e.dma_start(
                out=xt4[kk][:, :], in_=x_h[ii * 128:(ii + 1) * 128, :]), dsem)
        return dsem, gld

    def ln_stats(g, dsem, gld):
        em.wait("vector", dsem, gld)
        for k in range(4):
            vop(lambda e, sems, kk=k: e.reduce_sum(
                stat[:, kk:kk + 1], xt4[kk][:, :], AX.X))
        for k in range(4):
            em.wait("vector", "t", xn_t[k])
            em.wait("vector", stsem[k], xn_d[k])
            vop(lambda e, sems, kk=k: e.tensor_tensor(
                xn4[kk][:, :], xt4[kk][:, :], xt4[kk][:, :], ALU.mult))
        vdrain()
        for k in range(4):
            vop(lambda e, sems, kk=k: e.reduce_sum(
                stat[:, 4 + kk:5 + kk], xn4[kk][:, :], AX.X))
        vdrain()
        vop(lambda e, sems: e.tensor_scalar(
            stat[:, 0:4], stat[:, 0:4], kon[:, 1:2], None, ALU.mult))
        vop(lambda e, sems: e.tensor_scalar(
            stat[:, 4:8], stat[:, 4:8], kon[:, 1:2], None, ALU.mult))
        vdrain()
        vop(lambda e, sems: e.tensor_tensor(
            stat[:, 8:12], stat[:, 0:4], stat[:, 0:4], ALU.mult))
        vdrain()
        vop(lambda e, sems: e.tensor_sub(
            stat[:, 4:8], stat[:, 4:8], stat[:, 8:12]), inc="v")
        em.wait("scalar", "v", em.now("v"))
        sop(lambda e, sems: e.activation(
            stat[:, 8:12], stat[:, 4:8], AF.Sqrt, bias=kon[:, 2:3]), inc="s")
        em.wait("vector", "s", em.now("s"))
        vop(lambda e, sems: e.reciprocal(stat[:, 4:8], stat[:, 8:12]))
        vdrain()
        for k in range(4):
            i = 4 * g + k
            vop(lambda e, sems, kk=k: e.tensor_scalar(
                xn4[kk][:, :], xt4[kk][:, :], stat[:, kk:kk + 1],
                stat[:, 4 + kk:5 + kk], ALU.subtract, ALU.mult), inc="v")
            nrm[k] = em.now("v")
            xt_free[k] = nrm[k]
            em.wait("scalar", "v", nrm[k])
            xn_d[k] = em.dma("scalar", lambda e, sems, kk=k, ii=i: e.dma_start(
                out=xnsc[ii * 128:(ii + 1) * 128, :], in_=xn4[kk][:, :]),
                stsem[k])

    def ln_tiles(g):
        for k in range(4):
            i = 4 * g + k
            em.wait("tensor", "v", nrm[k])
            em.wait("tensor", "v", P1S["gx_free"])
            for c in range(NCD):
                top(lambda e, sems, kk=k, cc=c: e.matmul(
                    PS_X[:, cc:cc + 1], xn4[kk][:, cc * 128:(cc + 1) * 128],
                    kon[:, 0:1], start=True, stop=True),
                    inc="t" if c == NCD - 1 else None)
            t_g = em.now("t")
            em.wait("tensor", "v", P1S["tr_v"])
            for c4 in range(4):
                top(lambda e, sems, kk=k, cc=c4, q=c4: e.transpose(
                    PS_B[:, q * 128:(q + 1) * 128],
                    xn4[kk][:, cc * 128:(cc + 1) * 128], idf[:, :]),
                    inc="t" if c4 == 3 else None)
            t_b0 = em.now("t")
            em.wait("tensor", "s", P1S["tr_g"])
            for c4 in range(4):
                top(lambda e, sems, kk=k, c_=4 + c4, q=c4: e.transpose(
                    PS_B[:, 512 + q * 128: 512 + (q + 1) * 128],
                    xn4[kk][:, c_ * 128:(c_ + 1) * 128], idf[:, :]),
                    inc="t" if c4 == 3 else None)
            xn_t[k] = em.now("t")
            em.wait("vector", "t", t_g)
            vop(lambda e, sems: e.tensor_add(
                gsum[:, :], gsum[:, :], PS_X[:, 0:8]), inc="v")
            P1S["gx_free"] = em.now("v")
            em.wait("vector", "t", t_b0)
            for c4 in range(4):
                vop(lambda e, sems, cc=c4, q=c4, ii=i: e.tensor_copy(
                    xnT[:, cc * S + ii * 128: cc * S + (ii + 1) * 128],
                    PS_B[:, q * 128:(q + 1) * 128]),
                    inc="v" if c4 == 3 else None)
            P1S["tr_v"] = em.now("v")
            em.wait("scalar", "t", xn_t[k])
            for c4 in range(4):
                sop(lambda e, sems, c_=4 + c4, q=c4, ii=i: e.copy(
                    xnT[:, c_ * S + ii * 128: c_ * S + (ii + 1) * 128],
                    PS_B[:, 512 + q * 128: 512 + (q + 1) * 128]),
                    inc="s" if c4 == 3 else None)
            P1S["tr_g"] = em.now("s")

    def k_block(g):
        em.wait("tensor", "v", P1S["tr_v"])
        em.wait("tensor", "s", P1S["tr_g"])
        for m in range(NCD):
            par = P1S["kidx"] % 2
            if P1S["kidx"] >= 2:
                em.wait("tensor", "v", kev[0])
            for c in range(NCD):
                top(lambda e, sems, p=par, mm=m, cc=c, tt=g: e.matmul(
                    PS_A[:, p * 512:(p + 1) * 512],
                    wkbf[:, cc * 1024 + mm * 128: cc * 1024 + (mm + 1) * 128],
                    xnT[:, cc * S + tt * 512: cc * S + (tt + 1) * 512],
                    start=(cc == 0), stop=(cc == 7)),
                    inc="t" if c == 7 else None)
            t_k = em.now("t")
            em.wait("vector", "t", t_k)
            vop(lambda e, sems, p=par, mm=m, tt=g: e.tensor_scalar(
                KT[:, mm * S + tt * 512: mm * S + (tt + 1) * 512],
                PS_A[:, p * 512:(p + 1) * 512], bk_t[:, mm:mm + 1], None,
                ALU.add), inc="v")
            kev[0], kev[1] = kev[1], em.now("v")
            P1S["kidx"] += 1

    dg = ln_loads(0)
    ln_stats(0, *dg)
    for g in range(8):
        if g + 1 < 8:
            dg = ln_loads(g + 1)
        ln_tiles(g)
        if g + 1 < 8:
            ln_stats(g + 1, *dg)
        k_block(g)

    gc1 = arf(0, 4096)
    gc2 = arf(4096, 1024)
    gc3 = arf(5120, 2)
    em.wait("sync", "v", em.now("v"))
    em.wait("sync", "t", em.now("t"))
    for k in range(4):
        em.wait("sync", stsem[k], xn_d[k])
    ld(gc1.rearrange("p (c m) -> p c m", m=512),
       gw1_h.rearrange("(c p) m -> p c m", p=128))
    ld(gc2.rearrange("p (c m) -> p c m", m=256),
       gw2_h.rearrange("(c p) m -> p c m", p=128))
    d_gc = ld(gc3, gw3_h[:, :])

    vdrain()
    vop(lambda e, sems: e.tensor_scalar(
        gcol[:, :], gsum[:, :], kon[:, 4:5], None, ALU.mult))
    v_gcol = vflush()
    em.wait("tensor", "v", v_gcol)
    em.wait("tensor", "dc", d_gc)
    g1bank = [PS_A[:, 0:1], PS_A[:, 512:513], PS_B[:, 0:1], PS_B[:, 512:513]]
    for m in range(4):
        for c in range(NCD):
            top(lambda e, sems, mm=m, cc=c: e.matmul(
                g1bank[mm], gc1[:, cc * 512 + mm * 128: cc * 512 + (mm + 1) * 128],
                gcol[:, cc:cc + 1], start=(cc == 0), stop=(cc == 7)),
                inc="t" if (m == 3 and c == 7) else None)
    t_g1 = em.now("t")
    em.wait("scalar", "t", t_g1)
    for m in range(4):
        sop(lambda e, sems, mm=m: e.activation(
            gch1[:, mm:mm + 1], g1bank[mm], AF.Gelu, bias=gb1[:, mm:mm + 1]))
    s_g1 = sflush()
    em.wait("tensor", "s", s_g1)
    g2bank = [PS_B[:, 1024:1025], PS_B[:, 1536:1537]]
    for m in range(2):
        for c in range(4):
            top(lambda e, sems, mm=m, cc=c: e.matmul(
                g2bank[mm], gc2[:, cc * 256 + mm * 128: cc * 256 + (mm + 1) * 128],
                gch1[:, cc:cc + 1], start=(cc == 0), stop=(cc == 3)),
                inc="t" if (m == 1 and c == 3) else None)
    t_g2 = em.now("t")
    em.wait("scalar", "t", t_g2)
    for m in range(2):
        sop(lambda e, sems, mm=m: e.activation(
            gch2[:, mm:mm + 1], g2bank[mm], AF.Gelu, bias=gb2[:, mm:mm + 1]))
    s_g2 = sflush()
    em.wait("tensor", "s", s_g2)
    for c in range(2):
        top(lambda e, sems, cc=c: e.matmul(
            PT_F[0:1, 0:1], gc3[:, cc:cc + 1], gch2[:, cc:cc + 1],
            start=(cc == 0), stop=(cc == 1)), inc="t" if c == 1 else None)
    t_g3 = em.now("t")
    em.wait("scalar", "t", t_g3)
    sop(lambda e, sems: e.activation(
        sc11[0:1, 3:4], PT_F[0:1, 0:1], AF.Sigmoid, bias=sc11[0:1, 1:2]))
    s_gf = sflush()
    em.wait("vector", "s", s_gf)
    ops = [
        lambda e, sems: e.memset(sc11[0:1, 6:7], 0.7),
        lambda e, sems: e.memset(sc11[0:1, 7:8], -0.5),
        lambda e, sems: e.tensor_scalar(sc11[0:1, 6:7], sc11[0:1, 3:4],
                                        sc11[0:1, 6:7], sc11[0:1, 7:8],
                                        ALU.mult, ALU.add),
        lambda e, sems: e.tensor_tensor(sc11[0:1, 6:7], sc11[0:1, 6:7],
                                        sc11[0:1, 2:3], ALU.mult),
        lambda e, sems: e.memset(sc11[0:1, 7:8], 2.0),
        lambda e, sems: e.tensor_tensor(sc11[0:1, 6:7], sc11[0:1, 6:7],
                                        sc11[0:1, 7:8], ALU.mult),
        lambda e, sems: e.memset(sc11[0:1, 7:8], 0.1),
        lambda e, sems: e.tensor_tensor(sc11[0:1, 4:5], sc11[0:1, 6:7],
                                        sc11[0:1, 7:8], ALU.add),
        lambda e, sems: e.memset(sc11[0:1, 7:8], 0.6),
        lambda e, sems: e.tensor_tensor(sc11[0:1, 5:6], sc11[0:1, 2:3],
                                        sc11[0:1, 7:8], ALU.mult),
        lambda e, sems: e.tensor_scalar(sc11[0:1, 5:6], sc11[0:1, 5:6],
                                        kon[0:1, 5:6], None, ALU.mult),
        lambda e, sems: e.tensor_tensor(sc11[0:1, 4:5], sc11[0:1, 4:5],
                                        sc11[0:1, 5:6], ALU.add),
        lambda e, sems: e.tensor_scalar(sc11[0:1, 6:7], sc11[0:1, 0:1],
                                        kon[0:1, 5:6], None, ALU.mult),
    ]
    for f in ops:
        vdrain()
        vop(f)
    vop(lambda e, sems: e.drain(fusable=False), inc="v")

    vdrain()
    vop(lambda e, sems: e.tensor_copy(stat[0:1, 9:10], sc11[0:1, 5:6]))
    vop(lambda e, sems: e.tensor_copy(stat[0:1, 10:11], sc11[0:1, 4:5]))
    vop(lambda e, sems: e.drain(fusable=False), inc="v")
    em.wait("tensor", "v", em.now("v"))
    top(lambda e, sems: e.matmul(PT_F[0:2, 0:1], stat[0:1, 9:11],
                                 kon[0:1, 0:1], start=True, stop=True),
        inc="t")
    em.wait("vector", "t", em.now("t"))
    vop(lambda e, sems: e.tensor_copy(stat[0:2, 11:12], PT_F[0:2, 0:1]))
    vdrain()
    vop(lambda e, sems: e.tensor_scalar(
        ab2[0:2, :], bcr[0:2, 0:128], stat[0:2, 11:12], None, ALU.mult))
    vop(lambda e, sems: e.drain(fusable=False), inc="v")

    if statdump:
        v_dmp = em.vfl()
        em.wait("sync", "v", v_dmp)
        ld(sdmp_h[:, :], sdmp[:, :])

    em.barrier()
    if phases <= 2:
        return finish(nc, em, stack, SB)

    VST2 = 520
    Vbuf = arb(0, 32 * VST2)
    attnb = ARF[:, :].bitcast(BF16)
    wvb = [attnb[:, 0:4096], attnb[:, 4096:8192]]
    bvb = [attnb[:, 8192:8704], attnb[:, 8704:9216]]
    Pbb = attnb[:, 9216:11264]
    gostg = [attnb[:, 11264:12288], attnb[:, 12288:13312]]

    wvb_cast = [0, 0]
    vmm_done = [0, 0]
    av_last = 0
    psa_guard = ("s", 0)
    psb_free = ("s", 0)
    go_dma = [0, 0]
    rs_guard = 0
    bv_done = [0, 0]
    NQB2 = 16

    for hp in range(8 if not skip_attn else 0):
        p2 = hp % 2
        em.wait("gpsimd", "t", vmm_done[p2])
        wvb_cast[p2] = em.dma("gpsimd", lambda e, sems, h_=hp, p=p2: e.dma_start(
            out=wvb[p].rearrange("p (c n) -> p c n", n=512),
            in_=wv_h[:, h_ * 512:(h_ + 1) * 512]
            .rearrange("(c p) n -> p c n", p=128)), "gv0" if p2 == 0 else "gv1")
        em.wait("gpsimd", "v", bv_done[p2])
        d_bv = em.dma("gpsimd", lambda e, sems, h_=hp, p=p2: e.dma_start(
            out=bvb[p], in_=bvbc_h[:, h_ * 512:(h_ + 1) * 512]),
            "gb0" if p2 == 0 else "gb1")
        em.wait("tensor", "gv0" if p2 == 0 else "gv1", wvb_cast[p2])
        em.wait("vector", "gb0" if p2 == 0 else "gb1", d_bv)
        em.wait("vector", "t", av_last)
        vevs = {}
        for kc in range(32):
            pb = kc % 2
            sem, val = psa_guard
            em.wait("tensor", sem, val)
            if kc >= 2:
                em.wait("tensor", "v", vevs[kc - 1])
            for c in range(NCD):
                top(lambda e, sems, p=pb, cc=c, kk=kc, w=p2: e.matmul(
                    PS_A[:, p * 512:(p + 1) * 512],
                    xnT[:, cc * S + kk * 128: cc * S + (kk + 1) * 128],
                    wvb[w][:, cc * 512:(cc + 1) * 512],
                    start=(cc == 0), stop=(cc == 7)),
                    inc="t" if c == 7 else None)
            t_v = em.now("t")
            em.wait("vector", "t", t_v)
            vop(lambda e, sems, p=pb, kk=kc, w=p2: e.tensor_tensor(
                Vbuf[:, kk * VST2: kk * VST2 + 256],
                PS_A[:, p * 512: p * 512 + 256], bvb[w][:, 0:256], ALU.add))
            vevs[kc] = vop(lambda e, sems, p=pb, kk=kc, w=p2: e.tensor_tensor(
                Vbuf[:, kk * VST2 + 260: kk * VST2 + 516],
                PS_A[:, p * 512 + 256: p * 512 + 512], bvb[w][:, 256:512],
                ALU.add), inc="v")
        vmm_done[p2] = em.now("t")
        vop(lambda e, sems: e.memset(Vbuf[:, 256::VST2], 1.0))
        vop(lambda e, sems: e.memset(Vbuf[:, 516::VST2], 1.0))
        v_ones = em.vfl()
        psa_guard = ("v", v_ones)
        bv_done[p2] = em.now("v")

        first_av_of_pair = True
        for hh in range(2):
          poff = 64 * hh
          for qb in range(NQB if attn_mode >= 2 else 0):
            n_k = 4 * qb + 4
            score_done = {}
            exp_fl = {}
            tri_done = {}
            avd = {}

            def escores(kc):
                bank = kc % 4
                if kc >= 4:
                    em.wait("tensor", "s", exp_fl[(kc - 4) // 2])
                else:
                    sem, val = psa_guard
                    em.wait("tensor", sem, val)
                rr = kc - 4 * qb
                diag = 0 <= rr <= 3
                top(lambda e, sems, kk=kc, b=bank, h_=hp, po=poff, q=qb,
                    s1=not diag: e.matmul(
                    PS_A[:, b * 512:(b + 1) * 512],
                    KT[po:po + 64, h_ * S + kk * 128: h_ * S + (kk + 1) * 128],
                    xnT[po:po + 64, h_ * S + q * 512: h_ * S + (q + 1) * 512],
                    start=True, stop=s1), inc=None if diag else "t")
                if diag:
                    top(lambda e, sems, b=bank, r_=rr: e.matmul(
                        PS_A[:, b * 512 + r_ * 128: b * 512 + (r_ + 1) * 128],
                        idb[:, :], ngt[:, :], start=False, stop=True,
                        skip_group_check=True), inc="t")
                score_done[kc] = em.now("t")

            def eexp(g):
                bank0 = (2 * g) % 4
                em.wait("scalar", "t", score_done[2 * g + 1])
                if g >= 2 and (2 * g - 3) in avd:
                    em.wait("scalar", "t", avd[2 * g - 3])
                sop(lambda e, sems, b0=bank0: e.activation(
                    Pbb[:, b0 * 512: b0 * 512 + 1024],
                    PS_A[:, b0 * 512: b0 * 512 + 1024], AF.Exp,
                    bias=kon[:, 3:4]), inc="s")
                exp_fl[g] = em.now("s")

            def eav(kc):
                nonlocal first_av_of_pair
                bank = kc % 4
                u_min = max(0, kc - 4 * qb)
                em.wait("tensor", "s", exp_fl[kc // 2])
                if first_av_of_pair:
                    em.wait("tensor", "v", v_ones)
                    first_av_of_pair = False
                if kc == 0:
                    sem, val = psb_free
                    em.wait("tensor", sem, val)
                for u in range(u_min, 4):
                    top(lambda e, sems, kk=kc, b=bank, uu=u, q=qb, h2=hh:
                        e.matmul(
                        PS_B[:, uu * 512: uu * 512 + 257],
                        Pbb[:, b * 512 + uu * 128: b * 512 + (uu + 1) * 128],
                        Vbuf[:, kk * VST2 + h2 * 260: kk * VST2 + h2 * 260 + 257],
                        start=(kk == 0), stop=(kk == 4 * q + uu)),
                        inc="t" if u == 3 else None)
                avd[kc] = em.now("t")

            for g in range(n_k // 2):
                escores(2 * g)
                escores(2 * g + 1)
                eexp(g)
                if attn_mode >= 3 and g >= 1:
                    eav(2 * g - 2)
                    eav(2 * g - 1)
            if attn_mode >= 3:
                eav(n_k - 2)
                eav(n_k - 1)
                av_last = avd[n_k - 1]
            psa_guard = ("s", exp_fl[n_k // 2 - 1])
            em.wait("vector", "t", av_last)
            em.wait("vector", "s", rs_guard)
            vop(lambda e, sems: e.reciprocal(rs[:, 0:4], PS_B[:, 256::512]),
                inc="v")
            v_rs = em.now("v")
            em.wait("scalar", "v", v_rs)
            em.wait("scalar", "t", av_last)
            em.wait("scalar", "sgo0" if qb % 2 == 0 else "sgo1", go_dma[qb % 2])
            for u in range(4):
                sop(lambda e, sems, uu=u, p=qb % 2: e.activation(
                    gostg[p][:, uu * 256:(uu + 1) * 256],
                    PS_B[:, uu * 512: uu * 512 + 256], AF.Copy,
                    scale=rs[:, uu:uu + 1]), inc="s")
            psb_free = ("s", em.now("s"))
            rs_guard = psb_free[1]
            em.wait("scalar", "s", psb_free[1])
            go_dma[qb % 2] = em.dma(
                "scalar", lambda e, sems, q=qb, h_=2 * hp + hh, p=qb % 2:
                e.dma_start(
                    out=gosc[q * 512:(q + 1) * 512, h_ * 256:(h_ + 1) * 256]
                    .rearrange("(u p) n -> p u n", p=128),
                    in_=gostg[p][:, :].rearrange("p (u n) -> p u n", n=256)),
                "sgo0" if qb % 2 == 0 else "sgo1")

    em.barrier()
    if phases <= 3:
        return finish(nc, em, stack, SB)

    gbuf = None
    ggrp_dma = [[0] * 4, [0] * 4]
    ggel = [[0] * 4, [0] * 4]
    goT_free = [0, 0]

    def ggrp_sem(g):
        return "sh0" if g % 2 == 0 else "sh1"

    def emit_goT_dmas(qb):
        b = qb % 2
        em.wait("sync", "t", goT_free[b])
        for g in range(4):
            for c in range(g * 8, g * 8 + 8):
                ggrp_dma[b][g] = em.dma(
                    "sync", lambda e, sems, q=qb, cc=c, bb=b: e.dma_start(
                        out=gbuf[bb][:, cc * 512:(cc + 1) * 512],
                        in_=gosc[q * 512:(q + 1) * 512, cc * 128:(cc + 1) * 128],
                        transpose=True), ggrp_sem(g))

    def emit_goT_gelus(qb, groups):
        b = qb % 2
        for g in groups:
            em.wait("scalar", ggrp_sem(g), ggrp_dma[b][g])
            g_ = 0
            for c in range(g * 8, g * 8 + 8):
                g_ = sop(lambda e, sems, cc=c, bb=b: e.activation(
                    gbuf[bb][:, cc * 512:(cc + 1) * 512],
                    gbuf[bb][:, cc * 512:(cc + 1) * 512], AF.Gelu), inc="s")
            ggel[b][g] = g_

    cw1b = ar1(0, 8192)
    cw2b = ar1(8192, 12288 - 8192)
    cw3b = ar1(12288, 1024)
    cw4b = ar1(13312, 2)
    h1b = ar1(13440, 2048)
    h2b = ar1(15488, 1024)
    h3b = ar1(16512, 512)
    outcb = ar1(17024, 2048)

    xnf = [arf(0, 2048), arf(2048, 2048)]
    gT = arf(8192, 2048)
    gnat = arf(10240, 512)
    twtmp8 = AR1[:, 19072:23168].bitcast(F32)

    g_c1 = em.dma("gpsimd", lambda e, sems: e.dma_start(
        out=cw1b.rearrange("p (c m) -> p c m", m=1024),
        in_=cw1_h.rearrange("(c p) m -> p c m", p=128)), "gcn")
    g_c2 = em.dma("gpsimd", lambda e, sems: e.dma_start(
        out=cw2b.rearrange("p (c m) -> p c m", m=512),
        in_=cw2_h.rearrange("(c p) m -> p c m", p=128)), "gcn")
    g_c3 = em.dma("gpsimd", lambda e, sems: e.dma_start(
        out=cw3b.rearrange("p (c m) -> p c m", m=256),
        in_=cw3_h.rearrange("(c p) m -> p c m", p=128)), "gcn")
    g_c4 = em.dma("gpsimd", lambda e, sems: e.dma_start(
        out=cw4b, in_=cw4_h[:, :]), "gcn")
    em.wait("tensor", "gcn", g_c4)
    gbuf = [arb(0, 16384), xnT[:, 0:16384]]
    emit_goT_dmas(0)

    sgn = {0: (4, 1.0), 1: (5, 1.0), 2: (6, -1.0), 3: (7, -1.0),
           4: (0, -1.0), 5: (1, -1.0), 6: (2, 1.0), 7: (3, 1.0)}
    xnf_dma = [0, 0]
    xnf_free = [0, 0]
    tr_ev = [0, 0]
    hg = [0, 0]
    gb = [0]
    G = {"last_sig": 0, "last_bcast": 0, "ptf_free": 0, "outcb_free": 0,
         "gT_free": 0, "gnat_dma": 0}
    sA = {"outT": [arf(4096, 2048), arf(6144, 2048)], "buf": 0, "xp": 0,
          "dsem": "dx0", "fq": 0, "cast": 0, "sig": 0, "bcast": 0, "twist": 0,
          "tfT": 0}
    sB = {"outT": [AR1[:, 23168:27264].bitcast(F32),
                   AR1[:, 27264:31360].bitcast(F32)], "buf": 0, "xp": 1,
          "dsem": "dx1", "fq": 1, "cast": 0, "sig": 0, "bcast": 0, "twist": 0,
          "tfT": 0}

    def mm_group(lhsbuf, lstep, m_, ncc, rhs_fn, evdst, evoff, bias):
        b = gb[0] % 2
        gb[0] += 1
        em.wait("tensor", "s", hg[0])
        for c in range(ncc):
            l_ap = lhsbuf[:, c * lstep + m_ * 128: c * lstep + (m_ + 1) * 128]
            r_ap = rhs_fn(c)
            top(lambda e, sems, o=PS_A[:, b * 512: b * 512 + 256], l=l_ap,
                r=r_ap, s0=(c == 0), s1=(c == ncc - 1): e.matmul(
                o, l, r, start=s0, stop=s1),
                inc="t" if c == ncc - 1 else None)
        em.wait("scalar", "t", em.now("t"))
        ev = sop(lambda e, sems, o=evdst[:, evoff:evoff + 256],
                 i=PS_A[:, b * 512: b * 512 + 256], bi=bias: e.activation(
                 o, i, AF.Gelu, bias=bi), inc="s")
        hg[0], hg[1] = hg[1], ev

    def mm_h1(st, cyc):
        if cyc == 0:
            f = st["fq"]
            rhs = lambda c, f_=f: xnT[:, c * S + f_ * 256: c * S + (f_ + 1) * 256]
        else:
            em.wait("tensor", "v", st["cast"])
            rhs = lambda c: outcb[:, c * 256:(c + 1) * 256]
        for m_ in range(8):
            mm_group(cw1b, 1024, m_, 8, rhs, h1b, m_ * 256, cb1[:, m_:m_ + 1])
        G["outcb_free"] = em.now("t")

    def mm_rest(st):
        em.wait("tensor", "s", hg[1])
        for m_ in range(4):
            mm_group(cw2b, 512, m_, 8,
                     lambda c: h1b[:, c * 256:(c + 1) * 256],
                     h2b, m_ * 256, cb2[:, m_:m_ + 1])
        em.wait("tensor", "s", hg[1])
        for m_ in range(2):
            mm_group(cw3b, 256, m_, 4,
                     lambda c: h2b[:, c * 256:(c + 1) * 256],
                     h3b, m_ * 256, cb3[:, m_:m_ + 1])
        em.wait("tensor", "s", hg[1])
        em.wait("tensor", "s", G["last_sig"])
        for c in range(2):
            top(lambda e, sems, cc=c: e.matmul(
                PS_X[0:1, 0:256], cw4b[:, cc:cc + 1],
                h3b[:, cc * 256:(cc + 1) * 256],
                start=(cc == 0), stop=(cc == 1)), inc="t" if c == 1 else None)
        st["tfT"] = em.now("t")

    def tail_pre(st):
        em.wait("scalar", "t", st["tfT"])
        em.wait("scalar", "t", G["last_bcast"])
        st["sig"] = sop(lambda e, sems: e.activation(
            bcr[0:1, :], PS_X[0:1, 0:256], AF.Tanh,
            bias=sc11[0:1, 6:7], scale=kon[0:1, 5:6]), inc="s")
        G["last_sig"] = st["sig"]

    def tail_bcast(st):
        em.wait("tensor", "s", st["sig"])
        em.wait("tensor", "v", G["ptf_free"])
        top(lambda e, sems: e.matmul(PT_F[:, 0:256], ab2[0:2, :], bcr[0:2, :],
                                     start=True, stop=True), inc="t")
        st["bcast"] = em.now("t")
        G["last_bcast"] = st["bcast"]

    def tail_post(st, cyc):
        cur, nxt = st["buf"], 1 - st["buf"]
        oT = st["outT"]
        em.wait("vector", "t", st["bcast"])
        for j in range(8):
            srcc = sgn[j][0]
            vop(lambda e, sems, cu=cur, ss=srcc, jj=j, o=oT: e.tensor_tensor(
                twtmp8[:, jj * 256:(jj + 1) * 256], PT_F[:, 0:256],
                o[cu][:, ss * 256:(ss + 1) * 256], ALU.mult),
                inc="v" if j == 7 else None)
        G["ptf_free"] = em.now("v")
        vdrain()
        for j in range(8):
            sg_ = sgn[j][1]
            vop(lambda e, sems, cu=cur, nx=nxt, jj=j, g_=sg_, o=oT:
                e.tensor_tensor(o[nx][:, jj * 256:(jj + 1) * 256],
                                o[cu][:, jj * 256:(jj + 1) * 256],
                                twtmp8[:, jj * 256:(jj + 1) * 256],
                                ALU.add if g_ > 0 else ALU.subtract),
                inc="v" if (cyc == 2 and j == 7) else None)
        if cyc < 2:
            vdrain()
            em.wait("vector", "t", G["outcb_free"])
            vop(lambda e, sems, nx=nxt, o=oT: e.tensor_copy(
                outcb[:, :], o[nx][:, :]), inc="v")
            st["cast"] = em.now("v")
        else:
            st["twist"] = em.now("v")
        st["buf"] = nxt

    def gend(st):
        fq, cur, oT = st["fq"], st["buf"], st["outT"]
        em.wait("scalar", "v", st["twist"])
        em.wait("scalar", "t", G["gT_free"])
        s0 = sop(lambda e, sems, cu=cur, o=oT: e.activation(
            gT[:, :], o[cu][:, :], AF.Tanh, scale=kon[:, 5:6]), inc="s")
        em.wait("vector", "s", s0)
        vop(lambda e, sems: e.tensor_scalar(
            gT[:, :], gT[:, :], kon[:, 5:6], kon[:, 5:6], ALU.mult, ALU.add))
        vdrain()
        vop(lambda e, sems, cu=cur, o=oT: e.tensor_tensor(
            gT[:, :], o[cu][:, :], gT[:, :], ALU.mult), inc="v")
        em.wait("tensor", "v", em.now("v"))
        for s2 in range(2):
            for rnd in range(2):
                pb = rnd % 2
                bb = pb * 512
                em.wait("tensor", "v", tr_ev[pb])
                for q4 in range(4):
                    c = rnd * 4 + q4
                    top(lambda e, sems, ss=s2, cc=c, q=q4, b=bb: e.transpose(
                        PS_B[:, b + q * 128: b + (q + 1) * 128],
                        gT[:, cc * 256 + ss * 128: cc * 256 + ss * 128 + 128],
                        idf[:, :]), inc="t" if q4 == 3 else None)
                em.wait("vector", "t", em.now("t"))
                em.wait("vector", "sg", G["gnat_dma"])
                for q4 in range(4):
                    vop(lambda e, sems, q=q4, b=bb: e.tensor_copy(
                        gnat[:, q * 128:(q + 1) * 128],
                        PS_B[:, b + q * 128: b + (q + 1) * 128]),
                        inc="v" if q4 == 3 else None)
                tr_ev[pb] = em.now("v")
                em.wait("scalar", "v", tr_ev[pb])
                G["gnat_dma"] = em.dma(
                    "scalar", lambda e, sems, f=fq, ss=s2, r=rnd: e.dma_start(
                        out=gsc[f * 256 + ss * 128: f * 256 + (ss + 1) * 128,
                                r * 512:(r + 1) * 512],
                        in_=gnat[:, :]), "sg")
        G["gT_free"] = em.now("t")

    def out0_build2(st, fqn):
        p = st["xp"]
        dst = st["outT"][1 - st["buf"]]
        em.wait("sync", "t", xnf_free[p])
        xnf_dma[p] = em.dma("sync", lambda e, sems, f=fqn, pp=p: e.dma_start(
            out=xnf[pp][:, :].rearrange("p (s n) -> p s n", n=1024),
            in_=xnsc[f * 256:(f + 1) * 256, :].rearrange("(s p) n -> p s n",
                                                         p=128)), st["dsem"])
        em.wait("tensor", st["dsem"], xnf_dma[p])
        for rnd in range(4):
            pb = rnd % 2
            bb = pb * 512
            em.wait("tensor", "v", tr_ev[pb])
            for q4 in range(4):
                k = rnd * 4 + q4
                s2, c = k // 8, k % 8
                top(lambda e, sems, pp=p, ss=s2, cc=c, q=q4, b=bb: e.transpose(
                    PS_B[:, b + q * 128: b + (q + 1) * 128],
                    xnf[pp][:, ss * 1024 + cc * 128: ss * 1024 + (cc + 1) * 128],
                    idf[:, :]), inc="t" if q4 == 3 else None)
            em.wait("vector", "t", em.now("t"))
            for q4 in range(4):
                k = rnd * 4 + q4
                s2, c = k // 8, k % 8
                vop(lambda e, sems, ss=s2, cc=c, q=q4, b=bb, d=dst:
                    e.tensor_copy(
                    d[:, cc * 256 + ss * 128: cc * 256 + ss * 128 + 128],
                    PS_B[:, b + q * 128: b + (q + 1) * 128]),
                    inc="v" if q4 == 3 else None)
            tr_ev[pb] = em.now("v")
        xnf_free[p] = em.now("t")
        st["fq"] = fqn
        st["buf"] = 1 - st["buf"]

    out0_build2(sA, 0)
    out0_build2(sB, 1)
    for ss2 in range(NFQ // 2):
        for cyc in range(3):
            if cyc == 0:
                mm_h1(sA, 0); mm_rest(sA); tail_pre(sA)
            mm_h1(sB, cyc)
            tail_bcast(sA); tail_post(sA, cyc)
            mm_rest(sB); tail_pre(sB)
            if cyc < 2:
                mm_h1(sA, cyc + 1)
                tail_bcast(sB); tail_post(sB, cyc)
                mm_rest(sA); tail_pre(sA)
            else:
                gend(sA)
                tail_bcast(sB); tail_post(sB, 2)
                if ss2 + 1 < NFQ // 2:
                    out0_build2(sA, 2 * ss2 + 2)
                gend(sB)
                if ss2 + 1 < NFQ // 2:
                    out0_build2(sB, 2 * ss2 + 3)
        if ss2 == 5:
            emit_goT_gelus(0, (0, 1))
        if ss2 == 6:
            emit_goT_gelus(0, (2, 3))

    em.barrier()
    if phases <= 4:
        return finish(nc, em, stack, SB)

    sstg4 = arf(0, 4096)
    xnti4 = arf(4096, 4096)
    bfbc = arf(8192, 1024)
    tl4 = xnT[:, 16384:32768].bitcast(F32)
    gti4 = tl4[:, 0:4096]
    xti4 = tl4[:, 4096:8192]
    d_bf = em.dma("sync", lambda e, sems: e.dma_start(
        out=bfbc, in_=bfbc_h[:, :]), "dc")
    em.wait("vector", "dc", d_bf)

    bank8 = [PS_A[:, 0:512], PS_A[:, 512:1024],
             PS_B[:, 0:512], PS_B[:, 512:1024],
             PS_B[:, 1024:1536], PS_B[:, 1536:2048],
             PT_F[:, 0:512], PS_X[:, 0:512]]

    em.wait("gpsimd", "gwf", g_wf)
    wfgrp = [0] * 4
    for g4 in range(4):
        for c in range(g4 * 8, g4 * 8 + 8):
            wfgrp[g4] = em.dma("gpsimd", lambda e, sems, cc=c: e.dma_start(
                out=AR1[:, cc * 1024:(cc + 1) * 1024],
                in_=wfbf[cc * 128:(cc + 1) * 128, :]),
                "gv0" if g4 % 2 == 0 else "gv1")

    tile_free = [0, 0, 0, 0]
    usem = ["dt0", "dt1", "dx0", "dx1"]
    ev_done_qb = 0
    out_dma = [0, 0]

    def load_tiles(qb, u):
        em.wait("sync", "v", tile_free[u])
        r0 = qb * 512 + u * 128
        em.dma("sync", lambda e, sems, r=r0, uu=u: e.dma_start(
            out=gti4[:, uu * 1024:(uu + 1) * 1024], in_=gsc[r:r + 128, :]),
            usem[u])
        em.dma("sync", lambda e, sems, r=r0, uu=u: e.dma_start(
            out=xti4[:, uu * 1024:(uu + 1) * 1024], in_=x_h[r:r + 128, :]),
            usem[u])
        return em.dma("sync", lambda e, sems, r=r0, uu=u: e.dma_start(
            out=xnti4[:, uu * 1024:(uu + 1) * 1024], in_=xnsc[r:r + 128, :]),
            usem[u])

    for qb in range(NQB):
        b = qb % 2
        tile_dma = [load_tiles(qb, u) for u in range(4)]
        if qb + 1 < NQB:
            emit_goT_dmas(qb + 1)
        gel = list(ggel[b])
        em.wait("tensor", "v", ev_done_qb)
        for c in range(32):
            if qb == 0 and c % 8 == 0:
                em.wait("tensor", "gv0" if (c // 8) % 2 == 0 else "gv1",
                        wfgrp[c // 8])
            if c % 8 == 0:
                em.wait("tensor", "s", gel[c // 8])
            for u in range(4):
                for dh in range(2):
                    top(lambda e, sems, cc=c, uu=u, d_=dh, bb=b: e.matmul(
                        bank8[uu * 2 + d_],
                        gbuf[bb][:, cc * 512 + uu * 128: cc * 512 + (uu + 1) * 128],
                        AR1[:, cc * 1024 + d_ * 512: cc * 1024 + (d_ + 1) * 512],
                        start=(cc == 0), stop=(cc == 31)),
                        inc="t" if (u == 3 and dh == 1) else None)
        goT_free[b] = em.now("t")
        smm = em.now("t")
        em.wait("vector", "t", smm)
        em.wait("vector", "sout0", out_dma[0])
        em.wait("vector", "sout1", out_dma[1])
        for u in range(4):
            vop(lambda e, sems, uu=u: e.tensor_add(
                sstg4[:, uu * 1024: uu * 1024 + 512],
                bank8[uu * 2], bfbc[:, 0:512]))
            vop(lambda e, sems, uu=u: e.tensor_add(
                sstg4[:, uu * 1024 + 512: uu * 1024 + 1024],
                bank8[uu * 2 + 1], bfbc[:, 512:1024]))
        ev_done_qb = vflush()
        if qb + 1 < NQB:
            emit_goT_gelus(qb + 1, (0, 1, 2, 3))
        for u in range(4):
            em.wait("vector", usem[u], tile_dma[u])
            st = sstg4[:, u * 1024:(u + 1) * 1024]
            vop(lambda e, sems, uu=u, t_=st: e.tensor_add(
                t_, t_, xnti4[:, uu * 1024:(uu + 1) * 1024]))
            vdrain()
            vop(lambda e, sems, uu=u, t_=st: e.tensor_mul(
                t_, t_, gti4[:, uu * 1024:(uu + 1) * 1024]))
            vdrain()
            vop(lambda e, sems, uu=u, t_=st: e.tensor_add(
                t_, t_, xti4[:, uu * 1024:(uu + 1) * 1024]))
            v_out = vflush()
            tile_free[u] = v_out
            em.wait("scalar", "v", v_out)
            out_dma[u % 2] = em.dma("scalar", lambda e, sems, q=qb, uu=u,
                                    t_=st: e.dma_start(
                out=out_h[q * 512 + uu * 128: q * 512 + (uu + 1) * 128, :],
                in_=t_), "sout0" if u % 2 == 0 else "sout1")

    em.wait("scalar", "sout0", em.now("sout0"))
    em.wait("scalar", "sout1", em.now("sout1"))
    return finish(nc, em, stack, SB)


def finish(nc, em, stack, SB):
    sems = {}
    semstack = contextlib.ExitStack()
    for sname in SEMS:
        sems[sname] = semstack.enter_context(nc.semaphore(sname))
    with nc.Block() as block:
        def run(engobj, stream):
            for fn, inc, n in stream:
                instr = fn(engobj, sems)
                if inc is not None and instr is not None:
                    instr.then_inc(sems[inc], n)

        @block.sync
        def _(e):
            run(e, em.streams["sync"])

        @block.gpsimd
        def _(e):
            run(e, em.streams["gpsimd"])

        @block.vector
        def _(e):
            run(e, em.streams["vector"])

        @block.scalar
        def _(e):
            run(e, em.streams["scalar"])

        @block.tensor
        def _(e):
            run(e, em.streams["tensor"])
    semstack.close()
    stack.close()
    return nc



def _host_consts():
    identf = np.eye(128, dtype=np.float32)
    trimask = np.triu(np.ones((128, 128), np.float32)).astype(ml_dtypes.bfloat16)
    onesrow = np.ones((1, 128), np.float32)
    return identf, trimask, onesrow


def _prep(inputs):
    identf, trimask, onesrow = _host_consts()
    f = np.asarray
    com = {
        "wk": np.ascontiguousarray(f(inputs["L0_attn_Wk"], np.float32)),
        "wv": np.ascontiguousarray(f(inputs["L0_attn_Wv"], np.float32)),
        "wf": np.ascontiguousarray(f(inputs["L0_attn_Wf"], np.float32)),
        "bk_t": np.ascontiguousarray(
            f(inputs["L0_attn_bk"], np.float32).reshape(8, 128).T),
        "bv_bc": np.ascontiguousarray(np.broadcast_to(
            f(inputs["L0_attn_bv"], np.float32), (128, DHID))),
        "bf_bc": np.ascontiguousarray(np.broadcast_to(
            f(inputs["L0_attn_bf"], np.float32), (128, D))),
        "cn_w1": np.ascontiguousarray(f(inputs["L0_ffn_cn_w1"], np.float32)),
        "cn_w2": np.ascontiguousarray(f(inputs["L0_ffn_cn_w2"], np.float32)),
        "cn_w3": np.ascontiguousarray(f(inputs["L0_ffn_cn_w3"], np.float32)),
        "cn_w4t": np.ascontiguousarray(
            f(inputs["L0_ffn_cn_w4"], np.float32).reshape(2, 128).T),
        "cn_b1t": np.ascontiguousarray(
            f(inputs["L0_ffn_cn_b1"], np.float32).reshape(8, 128).T),
        "cn_b2t": np.ascontiguousarray(
            f(inputs["L0_ffn_cn_b2"], np.float32).reshape(4, 128).T),
        "cn_b3t": np.ascontiguousarray(
            f(inputs["L0_ffn_cn_b3"], np.float32).reshape(2, 128).T),
        "cn_b4s": f(inputs["L0_ffn_cn_b4"], np.float32).reshape(1, 1),
        "gc_w1": np.ascontiguousarray(f(inputs["L0_ffn_gc_w1"], np.float32)),
        "gc_w2": np.ascontiguousarray(f(inputs["L0_ffn_gc_w2"], np.float32)),
        "gc_w3t": np.ascontiguousarray(
            f(inputs["L0_ffn_gc_w3"], np.float32).reshape(2, 128).T),
        "gc_b1t": np.ascontiguousarray(
            f(inputs["L0_ffn_gc_b1"], np.float32).reshape(4, 128).T),
        "gc_b2t": np.ascontiguousarray(
            f(inputs["L0_ffn_gc_b2"], np.float32).reshape(2, 128).T),
        "gc_b3s": f(inputs["L0_ffn_gc_b3"], np.float32).reshape(1, 1),
        "ar_s": f(inputs["L0_ffn_adaptive_range"], np.float32).reshape(1, 1),
        "identf": identf,
        "trimask": trimask,
        "onesrow": onesrow,
        "bcr2i": np.ones((2, 256), np.float32),
    }
    x = f(inputs["L0_attn_x"], np.float32)
    maps = []
    for b in range(8):
        m = dict(com)
        m["x"] = np.ascontiguousarray(x[b])
        maps.append(m)
    return maps


_CACHE = {}


def kernel(**inputs):
    key = "prog"
    if key not in _CACHE:
        _CACHE[key] = build_program(debug=False, phases=5)
    nc = _CACHE[key]
    maps = _prep(inputs)
    res = run_bass_kernel_spmd(nc, maps, core_ids=list(range(8)))
    return np.stack([res.results[b]["out"] for b in range(8)], axis=0)


# version 23
# speedup vs baseline: 1.0887x; 1.0818x over previous
import contextlib
import numpy as np
import ml_dtypes
import concourse.bass as bass
import concourse.mybir as mybir
from concourse.bass_utils import run_bass_kernel_spmd

F32 = mybir.dt.float32
BF16 = mybir.dt.bfloat16
AF = mybir.ActivationFunctionType
ALU = mybir.AluOpType
AX = mybir.AxisListType

S, D, H, DH, DV, DHID = 4096, 1024, 16, 64, 256, 4096
NT, NQB, NFQ, NCD = 32, 8, 16, 8
VST = 260

ENGS = ("sync", "gpsimd", "vector", "scalar", "tensor")
SEMS = ("dc", "dx0", "dx1", "dt0", "dt1", "mks",
        "gwk", "gwf", "gcn", "gv0", "gv1", "gb0", "gb1", "mkg",
        "so0", "so1", "sg", "sgo0", "sgo1", "sh0", "sh1", "sout0", "sout1",
        "t", "v", "s", "g", "mkt")
QSEMS = {"sync": ("dc", "dx0", "dx1", "dt0", "dt1", "sh0", "sh1"),
         "gpsimd": ("gwk", "gwf", "gcn", "gv0", "gv1", "gb0", "gb1"),
         "scalar": ("so0", "so1", "sg", "sgo0", "sgo1", "sout0", "sout1")}


class Em:

    def __init__(self):
        self.streams = {e: [] for e in ENGS}
        self.counts = {k: 0 for k in SEMS}

    def op(self, eng, fn, inc=None, n=1):
        if inc is not None:
            self.counts[inc] += n
        self.streams[eng].append((fn, inc, n))
        return self.counts[inc] if inc else None

    def dma(self, eng, fn, sem):
        return self.op(eng, fn, inc=sem, n=16)

    def wait(self, eng, sem, val):
        if val and val > 0:
            self.streams[eng].append(
                (lambda e, sems, s=sem, v=val: e.wait_ge(sems[s], v), None, 0))

    def now(self, sem):
        return self.counts[sem]

    def mark(self, eng, mksem):
        for sname in QSEMS[eng]:
            self.wait(eng, sname, self.counts[sname])
        self.counts[mksem] += 16
        val = self.counts[mksem]
        self.streams[eng].append(
            (lambda e, sems, s=mksem: e.sem_inc(sems[s], 16), None, 0))
        return val

    def barrier(self):
        marks = {}
        for sname in QSEMS["scalar"]:
            self.wait("scalar", sname, self.counts[sname])
        for e in ("vector", "scalar", "tensor"):
            if e == "tensor":
                sn = "mkt"
                self.op(e, lambda en, sems: en.nop(nofuse=True), inc=sn)
            elif e == "vector":
                sn = "v"
                self.vfl()
            else:
                sn = "s"
                self.sfl()
            marks[e] = (sn, self.now(sn))
        marks["sync"] = ("mks", self.mark("sync", "mks"))
        marks["gpsimd"] = ("mkg", self.mark("gpsimd", "mkg"))
        for e in ENGS:
            for o in ENGS:
                if o != e:
                    self.wait(e, marks[o][0], marks[o][1])


def build_program(debug=False, phases=5, statdump=False, k_limit=64, skip_attn=False, attn_mode=3):
    nc = bass.Bass("TRN2", target_bir_lowering=False)

    def inp(name, shape, dt=F32):
        return nc.dram_tensor(name, list(shape), dt, kind="ExternalInput").ap()

    x_h = inp("x", (S, D))
    wk_h = inp("wk", (D, D))
    wv_h = inp("wv", (D, DHID))
    wf_h = inp("wf", (DHID, D))
    bk_h = inp("bk_t", (128, 8))
    bvbc_h = inp("bv_bc", (128, DHID))
    bfbc_h = inp("bf_bc", (128, D))
    cw1_h = inp("cn_w1", (D, D))
    cw2_h = inp("cn_w2", (D, 512))
    cw3_h = inp("cn_w3", (512, 256))
    cw4_h = inp("cn_w4t", (128, 2))
    cb1_h = inp("cn_b1t", (128, 8))
    cb2_h = inp("cn_b2t", (128, 4))
    cb3_h = inp("cn_b3t", (128, 2))
    cb4_h = inp("cn_b4s", (1, 1))
    gw1_h = inp("gc_w1", (D, 512))
    gw2_h = inp("gc_w2", (512, 256))
    gw3_h = inp("gc_w3t", (128, 2))
    gb1_h = inp("gc_b1t", (128, 4))
    gb2_h = inp("gc_b2t", (128, 2))
    gb3_h = inp("gc_b3s", (1, 1))
    ar_h = inp("ar_s", (1, 1))
    idf_h = inp("identf", (128, 128))
    tri_h = inp("trimask", (128, 128), BF16)
    onr_h = inp("onesrow", (1, 128))
    bcr2_h = inp("bcr2i", (2, 256))

    out_h = nc.dram_tensor("out", [S, D], F32, kind="ExternalOutput").ap()
    skind = "ExternalOutput" if debug else "Internal"
    xnsc = nc.dram_tensor("xn_sc", [S, D], F32, kind=skind).ap()
    gosc = nc.dram_tensor("go_sc", [S, DHID], BF16, kind=skind).ap()
    gsc = nc.dram_tensor("g_sc", [S, D], F32, kind=skind).ap()
    if statdump:
        sdmp_h = nc.dram_tensor("sdmp", [128, 256], F32, kind="ExternalOutput").ap()
    wfbf = nc.dram_tensor("wf_bf", [DHID, D], BF16, kind="Internal").ap()

    em = Em()
    em.vfl = None
    em.sfl = None
    stack = contextlib.ExitStack()
    SB = {}

    def sbuf(name, cols, dt=F32, rows=128):
        SB[name] = stack.enter_context(nc.sbuf_tensor(name, [rows, cols], dt))
        return SB[name]

    xnT = sbuf("xnT", NCD * S, BF16)
    AR1 = sbuf("AR1", 32768, BF16)
    ARF = sbuf("ARF", 11264, F32)
    ARB = sbuf("ARB", 16640, BF16)
    idf = sbuf("idf", 128)
    tri = sbuf("tri", 128, BF16)
    idb = sbuf("idb", 128, BF16)
    ngt = sbuf("ngt", 128, BF16)
    onr = sbuf("onr", 128, rows=1)
    bk_t = sbuf("bk", 8)
    cb1 = sbuf("cb1", 8); cb2 = sbuf("cb2", 4); cb3 = sbuf("cb3", 2)
    gb1 = sbuf("gb1", 4); gb2 = sbuf("gb2", 2)
    sc11 = sbuf("sc11", 8, rows=1)
    kon = sbuf("kon", 8)
    gsum = sbuf("gsum", 8)
    gcol = sbuf("gcol", 8)
    gch1 = sbuf("gch1", 4)
    gch2 = sbuf("gch2", 2)
    rs = sbuf("rs", 4)
    stat = sbuf("stat", 12)
    flc = sbuf("flc", 8)
    if statdump:
        sdmp = sbuf("sdmp_b", 256)

    PS_A = stack.enter_context(nc.psum_tensor("PS_A", [128, 2048], F32))
    PS_B = stack.enter_context(nc.psum_tensor("PS_B", [128, 2048], F32))
    PT_F = PS_A[:, 1024:1536]
    PS_X = PS_A[:, 1536:2048]

    def arf(off, n):
        return ARF[:, off:off + n]

    def arb(off, n):
        return ARB[:, off:off + n]

    def ar1(off, n):
        return AR1[:, off:off + n]

    bcr = ARF[0:2, 10752:11008]
    ab2 = ARF[0:2, 11008:11136]

    def ld(dst, src, sem="dc"):
        return em.dma("sync", lambda e, sems, d=dst, s=src:
                      e.dma_start(out=d, in_=s), sem)

    ld(idf[:, :], idf_h[:, :])
    ld(tri[:, :], tri_h[:, :])
    ld(onr[0:1, :], onr_h[:, :])
    ld(bk_t[:, :], bk_h[:, :])
    ld(cb1[:, :], cb1_h[:, :]); ld(cb2[:, :], cb2_h[:, :]); ld(cb3[:, :], cb3_h[:, :])
    ld(gb1[:, :], gb1_h[:, :]); ld(gb2[:, :], gb2_h[:, :])
    ld(sc11[0:1, 0:1], cb4_h[:, :])
    ld(sc11[0:1, 1:2], gb3_h[:, :])
    ld(bcr[0:2, :], bcr2_h[:, :])
    d_consts = ld(sc11[0:1, 2:3], ar_h[:, :])

    vop = lambda fn, inc=None: em.op("vector", fn, inc=inc)
    sop = lambda fn, inc=None: em.op("scalar", fn, inc=inc)
    top = lambda fn, inc=None: em.op("tensor", fn, inc=inc)
    gop = lambda fn, inc=None: em.op("gpsimd", fn, inc=inc)

    def vflush():
        return vop(lambda e, sems: e.drain(fusable=False), inc="v")

    def sflush():
        return sop(lambda e, sems: e.drain(fusable=False), inc="s")

    def vdrain():
        vop(lambda e, sems: e.drain(fusable=False))

    def sdrain():
        sop(lambda e, sems: e.drain(fusable=False))

    em.vfl = vflush
    em.sfl = sflush

    vop(lambda e, sems: e.memset(kon[:, 0:1], 1.0))
    vop(lambda e, sems: e.memset(kon[:, 1:2], 1.0 / 1024.0))
    vop(lambda e, sems: e.memset(kon[:, 2:3], 1e-5))
    vop(lambda e, sems: e.memset(kon[:, 3:4], -20.0))
    vop(lambda e, sems: e.memset(kon[:, 4:5], 1.0 / 4096.0))
    vop(lambda e, sems: e.memset(kon[:, 5:6], 0.5))
    vop(lambda e, sems: e.memset(kon[:, 6:7], 1e9))
    v_kon = vop(lambda e, sems: e.memset(gsum[:, :], 0.0), inc="v")

    wkbf = arb(0, 8192)
    g_wk = em.dma("gpsimd", lambda e, sems: e.dma_start(
        out=wkbf.rearrange("p (c m) -> p c m", m=1024),
        in_=wk_h.rearrange("(c p) m -> p c m", p=128)), "gwk")
    g_wf = em.dma("gpsimd", lambda e, sems: e.dma_start(
        out=wfbf[:, :], in_=wf_h[:, :]), "gwf")

    xt4 = [arf(i * 1024, 1024) for i in range(4)]
    xn4 = [arf(4096 + i * 1024, 1024) for i in range(4)]
    KT = AR1

    for e in ("vector", "scalar", "tensor", "gpsimd"):
        em.wait(e, "dc", d_consts)
    em.wait("vector", "v", v_kon)
    em.wait("tensor", "gwk", g_wk)
    vop(lambda e, sems: e.tensor_copy(idb[:, :], idf[:, :]))
    vop(lambda e, sems: e.tensor_scalar(
        ngt[:, :], tri[:, :], kon[:, 0:1], kon[:, 6:7],
        ALU.subtract, ALU.mult))

    stsem = ["so0", "so1", "sgo0", "sgo1"]
    P1S = {"tr_v": 0, "tr_g": 0, "gx_free": 0, "kidx": 0}
    xt_free = [0] * 4
    nrm = [0] * 4
    xn_t = [0] * 4
    xn_d = [0] * 4
    kev = [0, 0]

    def ln_loads(g):
        dsem = "dx0" if g % 2 == 0 else "dx1"
        gld = 0
        for k in range(4):
            i = 4 * g + k
            em.wait("sync", "v", xt_free[k])
            gld = em.dma("sync", lambda e, sems, kk=k, ii=i: e.dma_start(
                out=xt4[kk][:, :], in_=x_h[ii * 128:(ii + 1) * 128, :]), dsem)
        return dsem, gld

    def ln_stats(g, dsem, gld):
        em.wait("vector", dsem, gld)
        for k in range(4):
            vop(lambda e, sems, kk=k: e.reduce_sum(
                stat[:, kk:kk + 1], xt4[kk][:, :], AX.X))
        for k in range(4):
            em.wait("vector", "t", xn_t[k])
            em.wait("vector", stsem[k], xn_d[k])
            vop(lambda e, sems, kk=k: e.tensor_tensor(
                xn4[kk][:, :], xt4[kk][:, :], xt4[kk][:, :], ALU.mult))
        vdrain()
        for k in range(4):
            vop(lambda e, sems, kk=k: e.reduce_sum(
                stat[:, 4 + kk:5 + kk], xn4[kk][:, :], AX.X))
        vdrain()
        vop(lambda e, sems: e.tensor_scalar(
            stat[:, 0:4], stat[:, 0:4], kon[:, 1:2], None, ALU.mult))
        vop(lambda e, sems: e.tensor_scalar(
            stat[:, 4:8], stat[:, 4:8], kon[:, 1:2], None, ALU.mult))
        vdrain()
        vop(lambda e, sems: e.tensor_tensor(
            stat[:, 8:12], stat[:, 0:4], stat[:, 0:4], ALU.mult))
        vdrain()
        vop(lambda e, sems: e.tensor_sub(
            stat[:, 4:8], stat[:, 4:8], stat[:, 8:12]), inc="v")
        em.wait("scalar", "v", em.now("v"))
        sop(lambda e, sems: e.activation(
            stat[:, 8:12], stat[:, 4:8], AF.Sqrt, bias=kon[:, 2:3]), inc="s")
        em.wait("vector", "s", em.now("s"))
        vop(lambda e, sems: e.reciprocal(stat[:, 4:8], stat[:, 8:12]))
        vdrain()
        for k in range(4):
            i = 4 * g + k
            vop(lambda e, sems, kk=k: e.tensor_scalar(
                xn4[kk][:, :], xt4[kk][:, :], stat[:, kk:kk + 1],
                stat[:, 4 + kk:5 + kk], ALU.subtract, ALU.mult), inc="v")
            nrm[k] = em.now("v")
            xt_free[k] = nrm[k]
            em.wait("scalar", "v", nrm[k])
            xn_d[k] = em.dma("scalar", lambda e, sems, kk=k, ii=i: e.dma_start(
                out=xnsc[ii * 128:(ii + 1) * 128, :], in_=xn4[kk][:, :]),
                stsem[k])

    def ln_tiles(g):
        for k in range(4):
            i = 4 * g + k
            em.wait("tensor", "v", nrm[k])
            em.wait("tensor", "v", P1S["gx_free"])
            for c in range(NCD):
                top(lambda e, sems, kk=k, cc=c: e.matmul(
                    PS_X[:, cc:cc + 1], xn4[kk][:, cc * 128:(cc + 1) * 128],
                    kon[:, 0:1], start=True, stop=True),
                    inc="t" if c == NCD - 1 else None)
            t_g = em.now("t")
            em.wait("tensor", "v", P1S["tr_v"])
            for c4 in range(4):
                top(lambda e, sems, kk=k, cc=c4, q=c4: e.transpose(
                    PS_B[:, q * 128:(q + 1) * 128],
                    xn4[kk][:, cc * 128:(cc + 1) * 128], idf[:, :]),
                    inc="t" if c4 == 3 else None)
            t_b0 = em.now("t")
            em.wait("tensor", "s", P1S["tr_g"])
            for c4 in range(4):
                top(lambda e, sems, kk=k, c_=4 + c4, q=c4: e.transpose(
                    PS_B[:, 512 + q * 128: 512 + (q + 1) * 128],
                    xn4[kk][:, c_ * 128:(c_ + 1) * 128], idf[:, :]),
                    inc="t" if c4 == 3 else None)
            xn_t[k] = em.now("t")
            em.wait("vector", "t", t_g)
            vop(lambda e, sems: e.tensor_add(
                gsum[:, :], gsum[:, :], PS_X[:, 0:8]), inc="v")
            P1S["gx_free"] = em.now("v")
            em.wait("vector", "t", t_b0)
            for c4 in range(4):
                vop(lambda e, sems, cc=c4, q=c4, ii=i: e.tensor_copy(
                    xnT[:, cc * S + ii * 128: cc * S + (ii + 1) * 128],
                    PS_B[:, q * 128:(q + 1) * 128]),
                    inc="v" if c4 == 3 else None)
            P1S["tr_v"] = em.now("v")
            em.wait("scalar", "t", xn_t[k])
            for c4 in range(4):
                sop(lambda e, sems, c_=4 + c4, q=c4, ii=i: e.copy(
                    xnT[:, c_ * S + ii * 128: c_ * S + (ii + 1) * 128],
                    PS_B[:, 512 + q * 128: 512 + (q + 1) * 128]),
                    inc="s" if c4 == 3 else None)
            P1S["tr_g"] = em.now("s")

    def k_block(g):
        em.wait("tensor", "v", P1S["tr_v"])
        em.wait("tensor", "s", P1S["tr_g"])
        for m in range(NCD):
            par = P1S["kidx"] % 2
            if P1S["kidx"] >= 2:
                em.wait("tensor", "v", kev[0])
            for c in range(NCD):
                top(lambda e, sems, p=par, mm=m, cc=c, tt=g: e.matmul(
                    PS_A[:, p * 512:(p + 1) * 512],
                    wkbf[:, cc * 1024 + mm * 128: cc * 1024 + (mm + 1) * 128],
                    xnT[:, cc * S + tt * 512: cc * S + (tt + 1) * 512],
                    start=(cc == 0), stop=(cc == 7)),
                    inc="t" if c == 7 else None)
            t_k = em.now("t")
            em.wait("vector", "t", t_k)
            vop(lambda e, sems, p=par, mm=m, tt=g: e.tensor_scalar(
                KT[:, mm * S + tt * 512: mm * S + (tt + 1) * 512],
                PS_A[:, p * 512:(p + 1) * 512], bk_t[:, mm:mm + 1], None,
                ALU.add), inc="v")
            kev[0], kev[1] = kev[1], em.now("v")
            P1S["kidx"] += 1

    dg = ln_loads(0)
    ln_stats(0, *dg)
    for g in range(8):
        if g + 1 < 8:
            dg = ln_loads(g + 1)
        ln_tiles(g)
        if g + 1 < 8:
            ln_stats(g + 1, *dg)
        k_block(g)

    gc1 = arf(0, 4096)
    gc2 = arf(4096, 1024)
    gc3 = arf(5120, 2)
    em.wait("sync", "v", em.now("v"))
    em.wait("sync", "t", em.now("t"))
    for k in range(4):
        em.wait("sync", stsem[k], xn_d[k])
    ld(gc1.rearrange("p (c m) -> p c m", m=512),
       gw1_h.rearrange("(c p) m -> p c m", p=128))
    ld(gc2.rearrange("p (c m) -> p c m", m=256),
       gw2_h.rearrange("(c p) m -> p c m", p=128))
    d_gc = ld(gc3, gw3_h[:, :])

    vdrain()
    vop(lambda e, sems: e.tensor_scalar(
        gcol[:, :], gsum[:, :], kon[:, 4:5], None, ALU.mult))
    v_gcol = vflush()
    em.wait("tensor", "v", v_gcol)
    em.wait("tensor", "dc", d_gc)
    g1bank = [PS_A[:, 0:1], PS_A[:, 512:513], PS_B[:, 0:1], PS_B[:, 512:513]]
    for m in range(4):
        for c in range(NCD):
            top(lambda e, sems, mm=m, cc=c: e.matmul(
                g1bank[mm], gc1[:, cc * 512 + mm * 128: cc * 512 + (mm + 1) * 128],
                gcol[:, cc:cc + 1], start=(cc == 0), stop=(cc == 7)),
                inc="t" if (m == 3 and c == 7) else None)
    t_g1 = em.now("t")
    em.wait("scalar", "t", t_g1)
    for m in range(4):
        sop(lambda e, sems, mm=m: e.activation(
            gch1[:, mm:mm + 1], g1bank[mm], AF.Gelu, bias=gb1[:, mm:mm + 1]))
    s_g1 = sflush()
    em.wait("tensor", "s", s_g1)
    g2bank = [PS_B[:, 1024:1025], PS_B[:, 1536:1537]]
    for m in range(2):
        for c in range(4):
            top(lambda e, sems, mm=m, cc=c: e.matmul(
                g2bank[mm], gc2[:, cc * 256 + mm * 128: cc * 256 + (mm + 1) * 128],
                gch1[:, cc:cc + 1], start=(cc == 0), stop=(cc == 3)),
                inc="t" if (m == 1 and c == 3) else None)
    t_g2 = em.now("t")
    em.wait("scalar", "t", t_g2)
    for m in range(2):
        sop(lambda e, sems, mm=m: e.activation(
            gch2[:, mm:mm + 1], g2bank[mm], AF.Gelu, bias=gb2[:, mm:mm + 1]))
    s_g2 = sflush()
    em.wait("tensor", "s", s_g2)
    for c in range(2):
        top(lambda e, sems, cc=c: e.matmul(
            PT_F[0:1, 0:1], gc3[:, cc:cc + 1], gch2[:, cc:cc + 1],
            start=(cc == 0), stop=(cc == 1)), inc="t" if c == 1 else None)
    t_g3 = em.now("t")
    em.wait("scalar", "t", t_g3)
    sop(lambda e, sems: e.activation(
        sc11[0:1, 3:4], PT_F[0:1, 0:1], AF.Sigmoid, bias=sc11[0:1, 1:2]))
    s_gf = sflush()
    em.wait("vector", "s", s_gf)
    ops = [
        lambda e, sems: e.memset(sc11[0:1, 6:7], 0.7),
        lambda e, sems: e.memset(sc11[0:1, 7:8], -0.5),
        lambda e, sems: e.tensor_scalar(sc11[0:1, 6:7], sc11[0:1, 3:4],
                                        sc11[0:1, 6:7], sc11[0:1, 7:8],
                                        ALU.mult, ALU.add),
        lambda e, sems: e.tensor_tensor(sc11[0:1, 6:7], sc11[0:1, 6:7],
                                        sc11[0:1, 2:3], ALU.mult),
        lambda e, sems: e.memset(sc11[0:1, 7:8], 2.0),
        lambda e, sems: e.tensor_tensor(sc11[0:1, 6:7], sc11[0:1, 6:7],
                                        sc11[0:1, 7:8], ALU.mult),
        lambda e, sems: e.memset(sc11[0:1, 7:8], 0.1),
        lambda e, sems: e.tensor_tensor(sc11[0:1, 4:5], sc11[0:1, 6:7],
                                        sc11[0:1, 7:8], ALU.add),
        lambda e, sems: e.memset(sc11[0:1, 7:8], 0.6),
        lambda e, sems: e.tensor_tensor(sc11[0:1, 5:6], sc11[0:1, 2:3],
                                        sc11[0:1, 7:8], ALU.mult),
        lambda e, sems: e.tensor_scalar(sc11[0:1, 5:6], sc11[0:1, 5:6],
                                        kon[0:1, 5:6], None, ALU.mult),
        lambda e, sems: e.tensor_tensor(sc11[0:1, 4:5], sc11[0:1, 4:5],
                                        sc11[0:1, 5:6], ALU.add),
        lambda e, sems: e.tensor_scalar(sc11[0:1, 6:7], sc11[0:1, 0:1],
                                        kon[0:1, 5:6], None, ALU.mult),
    ]
    for f in ops:
        vdrain()
        vop(f)
    vop(lambda e, sems: e.drain(fusable=False), inc="v")

    vdrain()
    vop(lambda e, sems: e.tensor_copy(stat[0:1, 9:10], sc11[0:1, 5:6]))
    vop(lambda e, sems: e.tensor_copy(stat[0:1, 10:11], sc11[0:1, 4:5]))
    vop(lambda e, sems: e.drain(fusable=False), inc="v")
    em.wait("tensor", "v", em.now("v"))
    top(lambda e, sems: e.matmul(PT_F[0:2, 0:1], stat[0:1, 9:11],
                                 kon[0:1, 0:1], start=True, stop=True),
        inc="t")
    em.wait("vector", "t", em.now("t"))
    vop(lambda e, sems: e.tensor_copy(stat[0:2, 11:12], PT_F[0:2, 0:1]))
    vdrain()
    vop(lambda e, sems: e.tensor_scalar(
        ab2[0:2, :], bcr[0:2, 0:128], stat[0:2, 11:12], None, ALU.mult))
    vop(lambda e, sems: e.drain(fusable=False), inc="v")

    if statdump:
        v_dmp = em.vfl()
        em.wait("sync", "v", v_dmp)
        ld(sdmp_h[:, :], sdmp[:, :])

    em.barrier()
    if phases <= 2:
        return finish(nc, em, stack, SB)

    VST2 = 520
    Vbuf = arb(0, 32 * VST2)
    attnb = ARF[:, :].bitcast(BF16)
    wvb = [attnb[:, 0:4096], attnb[:, 4096:8192]]
    bvb = [attnb[:, 8192:8704], attnb[:, 8704:9216]]
    Pbb = attnb[:, 9216:11264]
    gostg = [attnb[:, 11264:12288], attnb[:, 12288:13312]]

    wvb_cast = [0, 0]
    vmm_done = [0, 0]
    av_last = 0
    psa_guard = ("s", 0)
    psb_free = ("s", 0)
    go_dma = [0, 0]
    pend_epi = [None]
    bv_done = [0, 0]
    NQB2 = 16

    for hp in range(8 if not skip_attn else 0):
        p2 = hp % 2
        em.wait("gpsimd", "t", vmm_done[p2])
        wvb_cast[p2] = em.dma("gpsimd", lambda e, sems, h_=hp, p=p2: e.dma_start(
            out=wvb[p].rearrange("p (c n) -> p c n", n=512),
            in_=wv_h[:, h_ * 512:(h_ + 1) * 512]
            .rearrange("(c p) n -> p c n", p=128)), "gv0" if p2 == 0 else "gv1")
        em.wait("gpsimd", "v", bv_done[p2])
        d_bv = em.dma("gpsimd", lambda e, sems, h_=hp, p=p2: e.dma_start(
            out=bvb[p], in_=bvbc_h[:, h_ * 512:(h_ + 1) * 512]),
            "gb0" if p2 == 0 else "gb1")
        em.wait("tensor", "gv0" if p2 == 0 else "gv1", wvb_cast[p2])
        em.wait("vector", "gb0" if p2 == 0 else "gb1", d_bv)
        em.wait("vector", "t", av_last)
        vevs = {}
        for kc in range(32):
            pb = kc % 2
            sem, val = psa_guard
            em.wait("tensor", sem, val)
            if kc >= 2:
                em.wait("tensor", "v", vevs[kc - 1])
            for c in range(NCD):
                top(lambda e, sems, p=pb, cc=c, kk=kc, w=p2: e.matmul(
                    PS_A[:, p * 512:(p + 1) * 512],
                    xnT[:, cc * S + kk * 128: cc * S + (kk + 1) * 128],
                    wvb[w][:, cc * 512:(cc + 1) * 512],
                    start=(cc == 0), stop=(cc == 7)),
                    inc="t" if c == 7 else None)
            t_v = em.now("t")
            em.wait("vector", "t", t_v)
            vop(lambda e, sems, p=pb, kk=kc, w=p2: e.tensor_tensor(
                Vbuf[:, kk * VST2: kk * VST2 + 256],
                PS_A[:, p * 512: p * 512 + 256], bvb[w][:, 0:256], ALU.add))
            vevs[kc] = vop(lambda e, sems, p=pb, kk=kc, w=p2: e.tensor_tensor(
                Vbuf[:, kk * VST2 + 260: kk * VST2 + 516],
                PS_A[:, p * 512 + 256: p * 512 + 512], bvb[w][:, 256:512],
                ALU.add), inc="v")
        vmm_done[p2] = em.now("t")
        vop(lambda e, sems: e.memset(Vbuf[:, 256::VST2], 1.0))
        vop(lambda e, sems: e.memset(Vbuf[:, 516::VST2], 1.0))
        v_ones = em.vfl()
        psa_guard = ("v", v_ones)
        bv_done[p2] = em.now("v")

        first_av_of_pair = True
        for hh in range(2):
          poff = 64 * hh
          for qb in range(NQB if attn_mode >= 2 else 0):
            n_k = 4 * qb + 4
            score_done = {}
            exp_fl = {}
            tri_done = {}
            avd = {}

            def escores(kc):
                bank = kc % 4
                if kc >= 4:
                    em.wait("tensor", "s", exp_fl[(kc - 4) // 2])
                else:
                    sem, val = psa_guard
                    em.wait("tensor", sem, val)
                rr = kc - 4 * qb
                diag = 0 <= rr <= 3
                top(lambda e, sems, kk=kc, b=bank, h_=hp, po=poff, q=qb,
                    s1=not diag: e.matmul(
                    PS_A[:, b * 512:(b + 1) * 512],
                    KT[po:po + 64, h_ * S + kk * 128: h_ * S + (kk + 1) * 128],
                    xnT[po:po + 64, h_ * S + q * 512: h_ * S + (q + 1) * 512],
                    start=True, stop=s1), inc=None if diag else "t")
                if diag:
                    top(lambda e, sems, b=bank, r_=rr: e.matmul(
                        PS_A[:, b * 512 + r_ * 128: b * 512 + (r_ + 1) * 128],
                        idb[:, :], ngt[:, :], start=False, stop=True,
                        skip_group_check=True), inc="t")
                score_done[kc] = em.now("t")

            def eexp(g):
                bank0 = (2 * g) % 4
                em.wait("scalar", "t", score_done[2 * g + 1])
                if g >= 2 and (2 * g - 3) in avd:
                    em.wait("scalar", "t", avd[2 * g - 3])
                sop(lambda e, sems, b0=bank0: e.activation(
                    Pbb[:, b0 * 512: b0 * 512 + 1024],
                    PS_A[:, b0 * 512: b0 * 512 + 1024], AF.Exp,
                    bias=kon[:, 3:4]), inc="s")
                exp_fl[g] = em.now("s")

            def eav(kc):
                nonlocal first_av_of_pair
                bank = kc % 4
                u_min = max(0, kc - 4 * qb)
                em.wait("tensor", "s", exp_fl[kc // 2])
                if first_av_of_pair:
                    em.wait("tensor", "v", v_ones)
                    first_av_of_pair = False
                if kc == 0:
                    sem, val = psb_free
                    em.wait("tensor", sem, val)
                for u in range(u_min, 4):
                    top(lambda e, sems, kk=kc, b=bank, uu=u, q=qb, h2=hh:
                        e.matmul(
                        PS_B[:, uu * 512: uu * 512 + 257],
                        Pbb[:, b * 512 + uu * 128: b * 512 + (uu + 1) * 128],
                        Vbuf[:, kk * VST2 + h2 * 260: kk * VST2 + h2 * 260 + 257],
                        start=(kk == 0), stop=(kk == 4 * q + uu)),
                        inc="t" if u == 3 else None)
                avd[kc] = em.now("t")

            for g in range(n_k // 2):
                escores(2 * g)
                escores(2 * g + 1)
                eexp(g)
                if g == 0 and pend_epi[0] is not None:
                    pend_epi[0]()
                    pend_epi[0] = None
                if attn_mode >= 3 and g >= 1:
                    eav(2 * g - 2)
                    eav(2 * g - 1)
            if attn_mode >= 3:
                eav(n_k - 2)
                eav(n_k - 1)
                av_last = avd[n_k - 1]
            psa_guard = ("s", exp_fl[n_k // 2 - 1])

            def make_epi(avl, q_, h_):
                def emit():
                    nonlocal psb_free
                    p = q_ % 2
                    em.wait("vector", "t", avl)
                    vop(lambda e, sems: e.reciprocal(
                        rs[:, 0:4], PS_B[:, 256::512]))
                    vdrain()
                    em.wait("vector", "sgo0" if p == 0 else "sgo1", go_dma[p])
                    for u in range(4):
                        vop(lambda e, sems, uu=u: e.tensor_scalar(
                            gostg[p][:, uu * 256:(uu + 1) * 256],
                            PS_B[:, uu * 512: uu * 512 + 256],
                            rs[:, uu:uu + 1], None, ALU.mult))
                    psb_free = ("v", vflush())
                    em.wait("scalar", "v", psb_free[1])
                    go_dma[p] = em.dma(
                        "scalar", lambda e, sems, q=q_, hx=h_, pp=p:
                        e.dma_start(
                            out=gosc[q * 512:(q + 1) * 512,
                                     hx * 256:(hx + 1) * 256]
                            .rearrange("(u p) n -> p u n", p=128),
                            in_=gostg[pp][:, :].rearrange("p (u n) -> p u n",
                                                          n=256)),
                        "sgo0" if p == 0 else "sgo1")
                return emit

            pend_epi[0] = make_epi(av_last, qb, 2 * hp + hh)

    if pend_epi[0] is not None:
        pend_epi[0]()
        pend_epi[0] = None
    em.barrier()
    if phases <= 3:
        return finish(nc, em, stack, SB)

    gbuf = None
    ggrp_dma = [[0] * 4, [0] * 4]
    ggel = [[0] * 4, [0] * 4]
    goT_free = [0, 0]

    def ggrp_sem(g):
        return "sh0" if g % 2 == 0 else "sh1"

    def emit_goT_dmas(qb):
        b = qb % 2
        em.wait("sync", "t", goT_free[b])
        for g in range(4):
            for c in range(g * 8, g * 8 + 8):
                ggrp_dma[b][g] = em.dma(
                    "sync", lambda e, sems, q=qb, cc=c, bb=b: e.dma_start(
                        out=gbuf[bb][:, cc * 512:(cc + 1) * 512],
                        in_=gosc[q * 512:(q + 1) * 512, cc * 128:(cc + 1) * 128],
                        transpose=True), ggrp_sem(g))

    def emit_goT_gelus(qb, groups):
        b = qb % 2
        for g in groups:
            em.wait("scalar", ggrp_sem(g), ggrp_dma[b][g])
            g_ = 0
            for c in range(g * 8, g * 8 + 8):
                g_ = sop(lambda e, sems, cc=c, bb=b: e.activation(
                    gbuf[bb][:, cc * 512:(cc + 1) * 512],
                    gbuf[bb][:, cc * 512:(cc + 1) * 512], AF.Gelu), inc="s")
            ggel[b][g] = g_

    cw1b = ar1(0, 8192)
    cw2b = ar1(8192, 12288 - 8192)
    cw3b = ar1(12288, 1024)
    cw4b = ar1(13312, 2)
    h1b = ar1(13440, 2048)
    h2b = ar1(15488, 1024)
    h3b = ar1(16512, 512)
    outcb = ar1(17024, 2048)

    xnf = [arf(0, 2048), arf(2048, 2048)]
    gT = arf(8192, 2048)
    gnat = arf(10240, 512)
    twtmp8 = AR1[:, 19072:23168].bitcast(F32)

    g_c1 = em.dma("gpsimd", lambda e, sems: e.dma_start(
        out=cw1b.rearrange("p (c m) -> p c m", m=1024),
        in_=cw1_h.rearrange("(c p) m -> p c m", p=128)), "gcn")
    g_c2 = em.dma("gpsimd", lambda e, sems: e.dma_start(
        out=cw2b.rearrange("p (c m) -> p c m", m=512),
        in_=cw2_h.rearrange("(c p) m -> p c m", p=128)), "gcn")
    g_c3 = em.dma("gpsimd", lambda e, sems: e.dma_start(
        out=cw3b.rearrange("p (c m) -> p c m", m=256),
        in_=cw3_h.rearrange("(c p) m -> p c m", p=128)), "gcn")
    g_c4 = em.dma("gpsimd", lambda e, sems: e.dma_start(
        out=cw4b, in_=cw4_h[:, :]), "gcn")
    em.wait("tensor", "gcn", g_c4)
    gbuf = [arb(0, 16384), xnT[:, 0:16384]]
    emit_goT_dmas(0)

    sgn = {0: (4, 1.0), 1: (5, 1.0), 2: (6, -1.0), 3: (7, -1.0),
           4: (0, -1.0), 5: (1, -1.0), 6: (2, 1.0), 7: (3, 1.0)}
    xnf_dma = [0, 0]
    xnf_free = [0, 0]
    tr_ev = [0, 0]
    hg = [0, 0]
    gb = [0]
    G = {"last_sig": 0, "last_bcast": 0, "ptf_free": 0, "outcb_free": 0,
         "gT_free": 0, "gnat_dma": 0}
    sA = {"outT": [arf(4096, 2048), arf(6144, 2048)], "buf": 0, "xp": 0,
          "dsem": "dx0", "fq": 0, "cast": 0, "sig": 0, "bcast": 0, "twist": 0,
          "tfT": 0}
    sB = {"outT": [AR1[:, 23168:27264].bitcast(F32),
                   AR1[:, 27264:31360].bitcast(F32)], "buf": 0, "xp": 1,
          "dsem": "dx1", "fq": 1, "cast": 0, "sig": 0, "bcast": 0, "twist": 0,
          "tfT": 0}

    def mm_group(lhsbuf, lstep, m_, ncc, rhs_fn, evdst, evoff, bias):
        b = gb[0] % 2
        gb[0] += 1
        em.wait("tensor", "s", hg[0])
        for c in range(ncc):
            l_ap = lhsbuf[:, c * lstep + m_ * 128: c * lstep + (m_ + 1) * 128]
            r_ap = rhs_fn(c)
            top(lambda e, sems, o=PS_A[:, b * 512: b * 512 + 256], l=l_ap,
                r=r_ap, s0=(c == 0), s1=(c == ncc - 1): e.matmul(
                o, l, r, start=s0, stop=s1),
                inc="t" if c == ncc - 1 else None)
        em.wait("scalar", "t", em.now("t"))
        ev = sop(lambda e, sems, o=evdst[:, evoff:evoff + 256],
                 i=PS_A[:, b * 512: b * 512 + 256], bi=bias: e.activation(
                 o, i, AF.Gelu, bias=bi), inc="s")
        hg[0], hg[1] = hg[1], ev

    def mm_h1(st, cyc):
        if cyc == 0:
            f = st["fq"]
            rhs = lambda c, f_=f: xnT[:, c * S + f_ * 256: c * S + (f_ + 1) * 256]
        else:
            em.wait("tensor", "v", st["cast"])
            rhs = lambda c: outcb[:, c * 256:(c + 1) * 256]
        for m_ in range(8):
            mm_group(cw1b, 1024, m_, 8, rhs, h1b, m_ * 256, cb1[:, m_:m_ + 1])
        G["outcb_free"] = em.now("t")

    def mm_rest(st):
        em.wait("tensor", "s", hg[1])
        for m_ in range(4):
            mm_group(cw2b, 512, m_, 8,
                     lambda c: h1b[:, c * 256:(c + 1) * 256],
                     h2b, m_ * 256, cb2[:, m_:m_ + 1])
        em.wait("tensor", "s", hg[1])
        for m_ in range(2):
            mm_group(cw3b, 256, m_, 4,
                     lambda c: h2b[:, c * 256:(c + 1) * 256],
                     h3b, m_ * 256, cb3[:, m_:m_ + 1])
        em.wait("tensor", "s", hg[1])
        em.wait("tensor", "s", G["last_sig"])
        for c in range(2):
            top(lambda e, sems, cc=c: e.matmul(
                PS_X[0:1, 0:256], cw4b[:, cc:cc + 1],
                h3b[:, cc * 256:(cc + 1) * 256],
                start=(cc == 0), stop=(cc == 1)), inc="t" if c == 1 else None)
        st["tfT"] = em.now("t")

    def tail_pre(st):
        em.wait("scalar", "t", st["tfT"])
        em.wait("scalar", "t", G["last_bcast"])
        st["sig"] = sop(lambda e, sems: e.activation(
            bcr[0:1, :], PS_X[0:1, 0:256], AF.Tanh,
            bias=sc11[0:1, 6:7], scale=kon[0:1, 5:6]), inc="s")
        G["last_sig"] = st["sig"]

    def tail_bcast(st):
        em.wait("tensor", "s", st["sig"])
        em.wait("tensor", "v", G["ptf_free"])
        top(lambda e, sems: e.matmul(PT_F[:, 0:256], ab2[0:2, :], bcr[0:2, :],
                                     start=True, stop=True), inc="t")
        st["bcast"] = em.now("t")
        G["last_bcast"] = st["bcast"]

    def tail_post(st, cyc):
        cur, nxt = st["buf"], 1 - st["buf"]
        oT = st["outT"]
        em.wait("vector", "t", st["bcast"])
        for j in range(8):
            srcc = sgn[j][0]
            vop(lambda e, sems, cu=cur, ss=srcc, jj=j, o=oT: e.tensor_tensor(
                twtmp8[:, jj * 256:(jj + 1) * 256], PT_F[:, 0:256],
                o[cu][:, ss * 256:(ss + 1) * 256], ALU.mult),
                inc="v" if j == 7 else None)
        G["ptf_free"] = em.now("v")
        vdrain()
        for j in range(8):
            sg_ = sgn[j][1]
            vop(lambda e, sems, cu=cur, nx=nxt, jj=j, g_=sg_, o=oT:
                e.tensor_tensor(o[nx][:, jj * 256:(jj + 1) * 256],
                                o[cu][:, jj * 256:(jj + 1) * 256],
                                twtmp8[:, jj * 256:(jj + 1) * 256],
                                ALU.add if g_ > 0 else ALU.subtract),
                inc="v" if (cyc == 2 and j == 7) else None)
        if cyc < 2:
            vdrain()
            em.wait("vector", "t", G["outcb_free"])
            vop(lambda e, sems, nx=nxt, o=oT: e.tensor_copy(
                outcb[:, :], o[nx][:, :]), inc="v")
            st["cast"] = em.now("v")
        else:
            st["twist"] = em.now("v")
        st["buf"] = nxt

    def gend(st):
        fq, cur, oT = st["fq"], st["buf"], st["outT"]
        em.wait("scalar", "v", st["twist"])
        em.wait("scalar", "t", G["gT_free"])
        s0 = sop(lambda e, sems, cu=cur, o=oT: e.activation(
            gT[:, :], o[cu][:, :], AF.Tanh, scale=kon[:, 5:6]), inc="s")
        em.wait("vector", "s", s0)
        vop(lambda e, sems: e.tensor_scalar(
            gT[:, :], gT[:, :], kon[:, 5:6], kon[:, 5:6], ALU.mult, ALU.add))
        vdrain()
        vop(lambda e, sems, cu=cur, o=oT: e.tensor_tensor(
            gT[:, :], o[cu][:, :], gT[:, :], ALU.mult), inc="v")
        em.wait("tensor", "v", em.now("v"))
        for s2 in range(2):
            for rnd in range(2):
                pb = rnd % 2
                bb = pb * 512
                em.wait("tensor", "v", tr_ev[pb])
                for q4 in range(4):
                    c = rnd * 4 + q4
                    top(lambda e, sems, ss=s2, cc=c, q=q4, b=bb: e.transpose(
                        PS_B[:, b + q * 128: b + (q + 1) * 128],
                        gT[:, cc * 256 + ss * 128: cc * 256 + ss * 128 + 128],
                        idf[:, :]), inc="t" if q4 == 3 else None)
                em.wait("vector", "t", em.now("t"))
                em.wait("vector", "sg", G["gnat_dma"])
                for q4 in range(4):
                    vop(lambda e, sems, q=q4, b=bb: e.tensor_copy(
                        gnat[:, q * 128:(q + 1) * 128],
                        PS_B[:, b + q * 128: b + (q + 1) * 128]),
                        inc="v" if q4 == 3 else None)
                tr_ev[pb] = em.now("v")
                em.wait("scalar", "v", tr_ev[pb])
                G["gnat_dma"] = em.dma(
                    "scalar", lambda e, sems, f=fq, ss=s2, r=rnd: e.dma_start(
                        out=gsc[f * 256 + ss * 128: f * 256 + (ss + 1) * 128,
                                r * 512:(r + 1) * 512],
                        in_=gnat[:, :]), "sg")
        G["gT_free"] = em.now("t")

    def out0_build2(st, fqn):
        p = st["xp"]
        dst = st["outT"][1 - st["buf"]]
        em.wait("sync", "t", xnf_free[p])
        xnf_dma[p] = em.dma("sync", lambda e, sems, f=fqn, pp=p: e.dma_start(
            out=xnf[pp][:, :].rearrange("p (s n) -> p s n", n=1024),
            in_=xnsc[f * 256:(f + 1) * 256, :].rearrange("(s p) n -> p s n",
                                                         p=128)), st["dsem"])
        em.wait("tensor", st["dsem"], xnf_dma[p])
        for rnd in range(4):
            pb = rnd % 2
            bb = pb * 512
            em.wait("tensor", "v", tr_ev[pb])
            for q4 in range(4):
                k = rnd * 4 + q4
                s2, c = k // 8, k % 8
                top(lambda e, sems, pp=p, ss=s2, cc=c, q=q4, b=bb: e.transpose(
                    PS_B[:, b + q * 128: b + (q + 1) * 128],
                    xnf[pp][:, ss * 1024 + cc * 128: ss * 1024 + (cc + 1) * 128],
                    idf[:, :]), inc="t" if q4 == 3 else None)
            em.wait("vector", "t", em.now("t"))
            for q4 in range(4):
                k = rnd * 4 + q4
                s2, c = k // 8, k % 8
                vop(lambda e, sems, ss=s2, cc=c, q=q4, b=bb, d=dst:
                    e.tensor_copy(
                    d[:, cc * 256 + ss * 128: cc * 256 + ss * 128 + 128],
                    PS_B[:, b + q * 128: b + (q + 1) * 128]),
                    inc="v" if q4 == 3 else None)
            tr_ev[pb] = em.now("v")
        xnf_free[p] = em.now("t")
        st["fq"] = fqn
        st["buf"] = 1 - st["buf"]

    out0_build2(sA, 0)
    out0_build2(sB, 1)
    for ss2 in range(NFQ // 2):
        for cyc in range(3):
            if cyc == 0:
                mm_h1(sA, 0); mm_rest(sA); tail_pre(sA)
            mm_h1(sB, cyc)
            tail_bcast(sA); tail_post(sA, cyc)
            mm_rest(sB); tail_pre(sB)
            if cyc < 2:
                mm_h1(sA, cyc + 1)
                tail_bcast(sB); tail_post(sB, cyc)
                mm_rest(sA); tail_pre(sA)
            else:
                gend(sA)
                tail_bcast(sB); tail_post(sB, 2)
                if ss2 + 1 < NFQ // 2:
                    out0_build2(sA, 2 * ss2 + 2)
                gend(sB)
                if ss2 + 1 < NFQ // 2:
                    out0_build2(sB, 2 * ss2 + 3)
        if ss2 == 5:
            emit_goT_gelus(0, (0, 1))
        if ss2 == 6:
            emit_goT_gelus(0, (2, 3))

    em.barrier()
    if phases <= 4:
        return finish(nc, em, stack, SB)

    sstg4 = arf(0, 4096)
    xnti4 = arf(4096, 4096)
    bfbc = arf(8192, 1024)
    tl4 = xnT[:, 16384:32768].bitcast(F32)
    gti4 = tl4[:, 0:4096]
    xti4 = tl4[:, 4096:8192]
    d_bf = em.dma("sync", lambda e, sems: e.dma_start(
        out=bfbc, in_=bfbc_h[:, :]), "dc")
    em.wait("vector", "dc", d_bf)

    bank8 = [PS_A[:, 0:512], PS_A[:, 512:1024],
             PS_B[:, 0:512], PS_B[:, 512:1024],
             PS_B[:, 1024:1536], PS_B[:, 1536:2048],
             PT_F[:, 0:512], PS_X[:, 0:512]]

    em.wait("gpsimd", "gwf", g_wf)
    wfgrp = [0] * 4
    for g4 in range(4):
        for c in range(g4 * 8, g4 * 8 + 8):
            wfgrp[g4] = em.dma("gpsimd", lambda e, sems, cc=c: e.dma_start(
                out=AR1[:, cc * 1024:(cc + 1) * 1024],
                in_=wfbf[cc * 128:(cc + 1) * 128, :]),
                "gv0" if g4 % 2 == 0 else "gv1")

    tile_free = [0, 0, 0, 0]
    usem = ["dt0", "dt1", "dx0", "dx1"]
    ev_done_qb = 0
    out_dma = [0, 0]

    def load_tiles(qb, u):
        em.wait("sync", "v", tile_free[u])
        r0 = qb * 512 + u * 128
        em.dma("sync", lambda e, sems, r=r0, uu=u: e.dma_start(
            out=gti4[:, uu * 1024:(uu + 1) * 1024], in_=gsc[r:r + 128, :]),
            usem[u])
        em.dma("sync", lambda e, sems, r=r0, uu=u: e.dma_start(
            out=xti4[:, uu * 1024:(uu + 1) * 1024], in_=x_h[r:r + 128, :]),
            usem[u])
        return em.dma("sync", lambda e, sems, r=r0, uu=u: e.dma_start(
            out=xnti4[:, uu * 1024:(uu + 1) * 1024], in_=xnsc[r:r + 128, :]),
            usem[u])

    for qb in range(NQB):
        b = qb % 2
        tile_dma = [load_tiles(qb, u) for u in range(4)]
        if qb + 1 < NQB:
            emit_goT_dmas(qb + 1)
        gel = list(ggel[b])
        em.wait("tensor", "v", ev_done_qb)
        for c in range(32):
            if qb == 0 and c % 8 == 0:
                em.wait("tensor", "gv0" if (c // 8) % 2 == 0 else "gv1",
                        wfgrp[c // 8])
            if c % 8 == 0:
                em.wait("tensor", "s", gel[c // 8])
            for u in range(4):
                for dh in range(2):
                    top(lambda e, sems, cc=c, uu=u, d_=dh, bb=b: e.matmul(
                        bank8[uu * 2 + d_],
                        gbuf[bb][:, cc * 512 + uu * 128: cc * 512 + (uu + 1) * 128],
                        AR1[:, cc * 1024 + d_ * 512: cc * 1024 + (d_ + 1) * 512],
                        start=(cc == 0), stop=(cc == 31)),
                        inc="t" if (u == 3 and dh == 1) else None)
        goT_free[b] = em.now("t")
        smm = em.now("t")
        em.wait("vector", "t", smm)
        em.wait("vector", "sout0", out_dma[0])
        em.wait("vector", "sout1", out_dma[1])
        for u in range(4):
            vop(lambda e, sems, uu=u: e.tensor_add(
                sstg4[:, uu * 1024: uu * 1024 + 512],
                bank8[uu * 2], bfbc[:, 0:512]))
            vop(lambda e, sems, uu=u: e.tensor_add(
                sstg4[:, uu * 1024 + 512: uu * 1024 + 1024],
                bank8[uu * 2 + 1], bfbc[:, 512:1024]))
        ev_done_qb = vflush()
        if qb + 1 < NQB:
            emit_goT_gelus(qb + 1, (0, 1, 2, 3))
        for u in range(4):
            em.wait("vector", usem[u], tile_dma[u])
            st = sstg4[:, u * 1024:(u + 1) * 1024]
            vop(lambda e, sems, uu=u, t_=st: e.tensor_add(
                t_, t_, xnti4[:, uu * 1024:(uu + 1) * 1024]))
            vdrain()
            vop(lambda e, sems, uu=u, t_=st: e.tensor_mul(
                t_, t_, gti4[:, uu * 1024:(uu + 1) * 1024]))
            vdrain()
            vop(lambda e, sems, uu=u, t_=st: e.tensor_add(
                t_, t_, xti4[:, uu * 1024:(uu + 1) * 1024]))
            v_out = vflush()
            tile_free[u] = v_out
            em.wait("scalar", "v", v_out)
            out_dma[u % 2] = em.dma("scalar", lambda e, sems, q=qb, uu=u,
                                    t_=st: e.dma_start(
                out=out_h[q * 512 + uu * 128: q * 512 + (uu + 1) * 128, :],
                in_=t_), "sout0" if u % 2 == 0 else "sout1")

    em.wait("scalar", "sout0", em.now("sout0"))
    em.wait("scalar", "sout1", em.now("sout1"))
    return finish(nc, em, stack, SB)


def finish(nc, em, stack, SB):
    sems = {}
    semstack = contextlib.ExitStack()
    for sname in SEMS:
        sems[sname] = semstack.enter_context(nc.semaphore(sname))
    with nc.Block() as block:
        def run(engobj, stream):
            for fn, inc, n in stream:
                instr = fn(engobj, sems)
                if inc is not None and instr is not None:
                    instr.then_inc(sems[inc], n)

        @block.sync
        def _(e):
            run(e, em.streams["sync"])

        @block.gpsimd
        def _(e):
            run(e, em.streams["gpsimd"])

        @block.vector
        def _(e):
            run(e, em.streams["vector"])

        @block.scalar
        def _(e):
            run(e, em.streams["scalar"])

        @block.tensor
        def _(e):
            run(e, em.streams["tensor"])
    semstack.close()
    stack.close()
    return nc



def _host_consts():
    identf = np.eye(128, dtype=np.float32)
    trimask = np.triu(np.ones((128, 128), np.float32)).astype(ml_dtypes.bfloat16)
    onesrow = np.ones((1, 128), np.float32)
    return identf, trimask, onesrow


def _prep(inputs):
    identf, trimask, onesrow = _host_consts()
    f = np.asarray
    com = {
        "wk": np.ascontiguousarray(f(inputs["L0_attn_Wk"], np.float32)),
        "wv": np.ascontiguousarray(f(inputs["L0_attn_Wv"], np.float32)),
        "wf": np.ascontiguousarray(f(inputs["L0_attn_Wf"], np.float32)),
        "bk_t": np.ascontiguousarray(
            f(inputs["L0_attn_bk"], np.float32).reshape(8, 128).T),
        "bv_bc": np.ascontiguousarray(np.broadcast_to(
            f(inputs["L0_attn_bv"], np.float32), (128, DHID))),
        "bf_bc": np.ascontiguousarray(np.broadcast_to(
            f(inputs["L0_attn_bf"], np.float32), (128, D))),
        "cn_w1": np.ascontiguousarray(f(inputs["L0_ffn_cn_w1"], np.float32)),
        "cn_w2": np.ascontiguousarray(f(inputs["L0_ffn_cn_w2"], np.float32)),
        "cn_w3": np.ascontiguousarray(f(inputs["L0_ffn_cn_w3"], np.float32)),
        "cn_w4t": np.ascontiguousarray(
            f(inputs["L0_ffn_cn_w4"], np.float32).reshape(2, 128).T),
        "cn_b1t": np.ascontiguousarray(
            f(inputs["L0_ffn_cn_b1"], np.float32).reshape(8, 128).T),
        "cn_b2t": np.ascontiguousarray(
            f(inputs["L0_ffn_cn_b2"], np.float32).reshape(4, 128).T),
        "cn_b3t": np.ascontiguousarray(
            f(inputs["L0_ffn_cn_b3"], np.float32).reshape(2, 128).T),
        "cn_b4s": f(inputs["L0_ffn_cn_b4"], np.float32).reshape(1, 1),
        "gc_w1": np.ascontiguousarray(f(inputs["L0_ffn_gc_w1"], np.float32)),
        "gc_w2": np.ascontiguousarray(f(inputs["L0_ffn_gc_w2"], np.float32)),
        "gc_w3t": np.ascontiguousarray(
            f(inputs["L0_ffn_gc_w3"], np.float32).reshape(2, 128).T),
        "gc_b1t": np.ascontiguousarray(
            f(inputs["L0_ffn_gc_b1"], np.float32).reshape(4, 128).T),
        "gc_b2t": np.ascontiguousarray(
            f(inputs["L0_ffn_gc_b2"], np.float32).reshape(2, 128).T),
        "gc_b3s": f(inputs["L0_ffn_gc_b3"], np.float32).reshape(1, 1),
        "ar_s": f(inputs["L0_ffn_adaptive_range"], np.float32).reshape(1, 1),
        "identf": identf,
        "trimask": trimask,
        "onesrow": onesrow,
        "bcr2i": np.ones((2, 256), np.float32),
    }
    x = f(inputs["L0_attn_x"], np.float32)
    maps = []
    for b in range(8):
        m = dict(com)
        m["x"] = np.ascontiguousarray(x[b])
        maps.append(m)
    return maps


_CACHE = {}


def kernel(**inputs):
    key = "prog"
    if key not in _CACHE:
        _CACHE[key] = build_program(debug=False, phases=5)
    nc = _CACHE[key]
    maps = _prep(inputs)
    res = run_bass_kernel_spmd(nc, maps, core_ids=list(range(8)))
    return np.stack([res.results[b]["out"] for b in range(8)], axis=0)
